# version 06
# baseline (speedup 1.0000x reference)
import numpy as np
from contextlib import ExitStack
import ml_dtypes

import concourse.bass as bass
import concourse.mybir as mybir
from concourse.bass_utils import run_bass_kernel_spmd

F32 = mybir.dt.float32
BF16 = mybir.dt.bfloat16
U8 = mybir.dt.uint8
AF = mybir.ActivationFunctionType
ALU = mybir.AluOpType
AX = mybir.AxisListType
NBF = ml_dtypes.bfloat16

B, T, D = 4, 4096, 2048
HID = 8192
NS, RK = 128, 128
E, HE, HS = 16, 1024, 4096
MTOK = 2048
TT = MTOK // 128
KT = D // 128
NCH = D // 512
HCHN = HID // 128
LN_EPS = 1e-5


def hadamard(n):
    h = np.array([[1.0]], np.float64)
    while h.shape[0] < n:
        h = np.block([[h, h], [h, -h]])
    return h


class Prog:
    ENGS = ("tensor", "vector", "scalar", "gpsimd", "sync")

    def __init__(self, nc, es):
        self.nc = nc
        self.es = es
        self.q = {e: [] for e in self.ENGS}
        self.sems = {}
        self.cnt = {}

    def sem(self, name):
        self.sems[name] = self.es.enter_context(self.nc.semaphore(name))
        self.cnt[name] = 0
        return self.sems[name]

    def do(self, eng, fn):
        self.q[eng].append(fn)

    def wait(self, eng, sname, val=None):
        v = self.cnt[sname] if val is None else val
        if v <= 0:
            return
        s = self.sems[sname]
        self.q[eng].append(lambda e, s=s, v=v: e.wait_ge(s, v))

    def inc(self, sname, k=1):
        self.cnt[sname] += k
        return self.sems[sname]

    def dma(self, eng, sname, fn):
        if self.cnt[sname] > 0:
            self.wait(eng, sname)
        sem = self.inc(sname, 16)
        self.do(eng, lambda e, fn=fn, sem=sem: fn(e).then_inc(sem, 16))

    OWN = {"vector": "vs", "scalar": "ss", "tensor": "ts"}

    def dma2(self, eng, sname, fn):
        sem = self.inc(sname, 16)
        self.do(eng, lambda e, fn=fn, sem=sem: fn(e).then_inc(sem, 16))

    def op(self, eng, sname, fn):
        own = self.OWN.get(eng)
        if own is not None and eng != "tensor" and self.cnt[own] > 0:
            self.wait(eng, own)
        sem = self.inc(sname, 1)
        self.do(eng, lambda e, fn=fn, sem=sem: fn(e).then_inc(sem, 1))

    def vop(self, fn):
        self.op("vector", "vs", fn)

    def sop(self, fn):
        self.op("scalar", "ss", fn)

    def attach(self, block):
        for eng in self.ENGS:
            q = self.q[eng]
            getattr(block, eng)(lambda e, q=q: [f(e) for f in q])


def reg_act_consts(nc, es, p, cstap, values):
    cb = es.enter_context(nc.sbuf_tensor("cstt", [128, len(values)], F32))
    for i, v in enumerate(values):
        p.dma("gpsimd", "ld", lambda g, i=i: g.dma_start(
            out=cb[:, i:i + 1], in_=cstap[:, i:i + 1].partition_broadcast(128)))
        nc.const_aps.register(F32, float(v), cb[:, i:i + 1])


def gemm16(bank, lhs_col, rhs_col, nk=KT):
    def run(te, bank=bank, lhs_col=lhs_col, rhs_col=rhs_col, nk=nk):
        for kt in range(nk):
            ins = te.matmul(bank[:], lhs_col(kt), rhs_col(kt),
                            start=(kt == 0), stop=(kt == nk - 1))
        return ins
    return run


def sbuf(nc, es, name, shape, dtype=F32):
    return es.enter_context(nc.sbuf_tensor(name, shape, dtype))


def psum(nc, es, name, shape, dtype=F32):
    return es.enter_context(nc.psum_tensor(name, shape, dtype))



def build_A():
    nc = bass.Bass("TRN2", target_bir_lowering=False)
    x0s = nc.dram_tensor("x0s", [MTOK, D], F32, kind="ExternalInput").ap()
    wzTb = nc.dram_tensor("wzTb", [KT, 128, D], F32, kind="ExternalInput").ap()
    whTb = nc.dram_tensor("whTb", [KT, 128, D], F32, kind="ExternalInput").ap()
    bzr = nc.dram_tensor("bzr", [1, D], F32, kind="ExternalInput").ap()
    bhr = nc.dram_tensor("bhr", [1, D], F32, kind="ExternalInput").ap()
    h0r = nc.dram_tensor("h0r", [1, D], F32, kind="ExternalInput").ap()
    w1Tb = nc.dram_tensor("w1Tb", [HCHN, KT, 128, 128], F32, kind="ExternalInput").ap()
    w3Tb = nc.dram_tensor("w3Tb", [HCHN, KT, 128, 128], F32, kind="ExternalInput").ap()
    w2T = nc.dram_tensor("w2T", [HID, D], F32, kind="ExternalInput").ap()
    h64s = nc.dram_tensor("h64s", [64, 64], F32, kind="ExternalInput").ap()
    h128 = nc.dram_tensor("h128", [128, 128], F32, kind="ExternalInput").ap()
    iden = nc.dram_tensor("iden", [128, 128], F32, kind="ExternalInput").ap()
    cst = nc.dram_tensor("cst", [1, 2], F32, kind="ExternalInput").ap()

    xn0_o = nc.dram_tensor("xn0_o", [MTOK, D], F32, kind="ExternalOutput").ap()
    vpr_o = nc.dram_tensor("vpr_o", [MTOK, D], F32, kind="ExternalOutput").ap()
    cumP_o = nc.dram_tensor("cumP_o", [MTOK, D], F32, kind="ExternalOutput").ap()
    v0_o = nc.dram_tensor("v0_o", [1, D], F32, kind="ExternalOutput").ap()
    ffn0_o = nc.dram_tensor("ffn0_o", [NCH, MTOK, 512], F32, kind="ExternalOutput").ap()

    k_hbm = nc.dram_tensor("k_hbm", [TT, 128, D], F32)
    hidA_hbm = nc.dram_tensor("hidA_hbm", [TT, HCHN, 128, 128], BF16)
    w2pp_hbm = nc.dram_tensor("w2pp_hbm", [NCH, HID, 512], BF16)

    x0_r = x0s.rearrange("(tt p) d -> tt p d", p=128)
    xn0_r = xn0_o.rearrange("(tt p) d -> tt p d", p=128)
    vpr_r = vpr_o.rearrange("(tt p) d -> tt p d", p=128)
    cumP_r = cumP_o.rearrange("(tt p) d -> tt p d", p=128)

    with ExitStack() as es:
        p = Prog(nc, es)
        for s in ("ld", "st", "ts", "vs", "ss", "lwa", "lwb", "lp", "lha", "lhb"):
            p.sem(s)

        xn0_fm = sbuf(nc, es, "xn0_fm", [128, KT * MTOK], BF16)
        wmat = sbuf(nc, es, "wmat", [128, KT * D], BF16)
        bzb = sbuf(nc, es, "bzb", [128, D], BF16)
        bhb = sbuf(nc, es, "bhb", [128, D], BF16)
        h128b = sbuf(nc, es, "h128b", [128, 128], BF16)
        h64f = sbuf(nc, es, "h64f", [64, 64], F32)
        idb = sbuf(nc, es, "idb", [128, 128], BF16)
        xb0 = sbuf(nc, es, "xb0", [128, D], F32)
        s1 = sbuf(nc, es, "s1", [128, D], F32)
        s2 = sbuf(nc, es, "s2", [128, D], F32)
        s3 = sbuf(nc, es, "s3", [128, D], F32)
        s4 = sbuf(nc, es, "s4", [128, D], F32)
        xnb = sbuf(nc, es, "xnb", [128, D], BF16)
        ksb = sbuf(nc, es, "ksb", [128, D], F32)
        ptb = sbuf(nc, es, "ptb", [128, D], F32)
        mask_t = sbuf(nc, es, "mask", [128, D], U8)
        red = sbuf(nc, es, "red", [128, 1], F32)
        red2 = sbuf(nc, es, "red2", [128, 1], F32)
        sgb = sbuf(nc, es, "sgb", [128, 512], BF16)
        ulb = sbuf(nc, es, "ulb", [128, 512], BF16)
        hidb = sbuf(nc, es, "hidb", [128, 512], BF16)
        hidAb = sbuf(nc, es, "hidAb", [128, 512], BF16)
        fout = sbuf(nc, es, "fout", [128, 512], F32)

        pG = [psum(nc, es, f"pG{i}", [128, 512], F32) for i in range(3)]
        pTr = psum(nc, es, "pTr", [128, 128], BF16)
        pF = psum(nc, es, "pF", [64, 512], F32)

        reg_act_consts(nc, es, p, cst, [-1.0, 0.5])
        p.dma("gpsimd", "ld", lambda g: g.dma_start(out=bzb[:], in_=bzr.partition_broadcast(128)))
        p.dma("gpsimd", "ld", lambda g: g.dma_start(out=bhb[:], in_=bhr.partition_broadcast(128)))
        p.dma("gpsimd", "ld", lambda g: g.dma_start(out=h128b[:], in_=h128))
        p.dma("gpsimd", "ld", lambda g: g.dma_start(out=h64f[:], in_=h64s))
        p.dma("gpsimd", "ld", lambda g: g.dma_start(out=idb[:], in_=iden))

        for tt in range(TT):
            p.wait("gpsimd", "vs")
            p.dma("gpsimd", "ld", lambda g, tt=tt: g.dma_start(out=xb0[:], in_=x0_r[tt]))
            p.wait("vector", "ld")
            p.wait("vector", "st")
            p.wait("vector", "ts")
            p.vop(lambda v: v.reduce_sum(out=red[:], in_=xb0[:], axis=AX.X))
            p.vop(lambda v: v.tensor_scalar(out=red[:], in0=red[:], scalar1=1.0 / D, scalar2=None, op0=ALU.mult))
            p.vop(lambda v: v.tensor_scalar(out=s1[:], in0=xb0[:], scalar1=red[:], scalar2=None, op0=ALU.subtract))
            p.wait("scalar", "vs")
            p.sop(lambda s: s.activation(out=s2[:], in_=s1[:], func=AF.Square, accum_out=red2[:]))
            p.wait("vector", "ss")
            p.vop(lambda v: v.tensor_scalar(out=red2[:], in0=red2[:], scalar1=1.0 / D, scalar2=LN_EPS, op0=ALU.mult, op1=ALU.add))
            p.wait("scalar", "vs")
            p.sop(lambda s: s.activation(out=red2[:], in_=red2[:], func=AF.Sqrt))
            p.wait("vector", "ss")
            p.vop(lambda v: v.reciprocal(out=red2[:], in_=red2[:]))
            p.vop(lambda v: v.tensor_scalar(out=s3[:], in0=s1[:], scalar1=red2[:], scalar2=None, op0=ALU.mult))
            p.vop(lambda v: v.tensor_copy(out=xnb[:], in_=s3[:]))
            p.wait("sync", "vs")
            p.dma("sync", "st", lambda sy, tt=tt: sy.dma_start(out=xn0_r[tt], in_=s3[:]))
            p.wait("tensor", "vs")
            for kt in range(KT):
                p.op("tensor", "ts", lambda te, kt=kt: te.transpose(pTr[:], xnb[:, kt * 128:(kt + 1) * 128], idb[:]))
                p.wait("vector", "ts")
                p.vop(lambda v, kt=kt, tt=tt: v.tensor_copy(
                    out=xn0_fm[:, kt * MTOK + tt * 128: kt * MTOK + (tt + 1) * 128], in_=pTr[:]))
                p.wait("tensor", "vs")

        for kt in range(KT):
            p.dma("gpsimd", "ld", lambda g, kt=kt: g.dma_start(out=wmat[:, kt * D:(kt + 1) * D], in_=wzTb[kt]))
        p.wait("tensor", "ld")

        for tt in range(TT):
            p.wait("vector", "st")
            for nch in range(NCH):
                bank = pG[nch % 3]
                p.wait("tensor", "vs")
                p.op("tensor", "ts", gemm16(
                    bank,
                    lambda kt, tt=tt: xn0_fm[:, kt * MTOK + tt * 128: kt * MTOK + (tt + 1) * 128],
                    lambda kt, nch=nch: wmat[:, kt * D + nch * 512: kt * D + (nch + 1) * 512]))
                p.wait("vector", "ts")
                p.vop(lambda v, bank=bank, nch=nch: v.tensor_tensor(
                    out=ksb[:, nch * 512:(nch + 1) * 512], in0=bank[:],
                    in1=bzb[:, nch * 512:(nch + 1) * 512], op=ALU.add))
            p.wait("sync", "vs")
            p.dma("sync", "st", lambda sy, tt=tt: sy.dma_start(out=k_hbm.ap()[tt], in_=ksb[:]))

        p.wait("gpsimd", "ts")
        for kt in range(KT):
            p.dma("gpsimd", "ld", lambda g, kt=kt: g.dma_start(out=wmat[:, kt * D:(kt + 1) * D], in_=whTb[kt]))

        p.dma("gpsimd", "ld", lambda g: g.dma_start(out=xb0[0:1, :], in_=h0r))
        p.wait("scalar", "ld")
        p.wait("scalar", "vs")
        p.sop(lambda s: s.activation(out=s1[0:1, :], in_=xb0[0:1, :], func=AF.Abs))
        p.sop(lambda s: s.activation(out=s1[0:1, :], in_=s1[0:1, :], func=AF.Exp, scale=-1.0))
        p.sop(lambda s: s.activation(out=s1[0:1, :], in_=s1[0:1, :], func=AF.Ln, bias=1.0))
        p.sop(lambda s: s.activation(out=s2[0:1, :], in_=xb0[0:1, :], func=AF.Relu))
        p.sop(lambda s: s.activation(out=s2[0:1, :], in_=s2[0:1, :], func=AF.Ln, bias=0.5))
        p.wait("vector", "ss")
        p.vop(lambda v: v.scalar_tensor_tensor(out=s1[0:1, :], in0=xb0[0:1, :], scalar=0.0, in1=s1[0:1, :], op0=ALU.min, op1=ALU.subtract))
        p.vop(lambda v: v.tensor_single_scalar(out=mask_t[0:1, :], in_=xb0[0:1, :], scalar=0.0, op=ALU.is_ge))
        p.vop(lambda v: v.select(out=s4[0:1, :], mask=mask_t[0:1, :], on_true=s2[0:1, :], on_false=s1[0:1, :]))
        p.wait("sync", "vs")
        p.dma("sync", "st", lambda sy: sy.dma_start(out=v0_o, in_=s4[0:1, :]))

        p.wait("tensor", "ld")
        for tt in range(TT):
            p.wait("vector", "ss")
            p.wait("vector", "st")
            for nch in range(NCH):
                bank = pG[nch % 3]
                p.wait("tensor", "vs")
                p.op("tensor", "ts", gemm16(
                    bank,
                    lambda kt, tt=tt: xn0_fm[:, kt * MTOK + tt * 128: kt * MTOK + (tt + 1) * 128],
                    lambda kt, nch=nch: wmat[:, kt * D + nch * 512: kt * D + (nch + 1) * 512]))
                p.wait("vector", "ts")
                p.vop(lambda v, bank=bank, nch=nch: v.tensor_tensor(
                    out=ptb[:, nch * 512:(nch + 1) * 512], in0=bank[:],
                    in1=bhb[:, nch * 512:(nch + 1) * 512], op=ALU.add))
            p.wait("gpsimd", "st")
            p.wait("gpsimd", "ss")
            p.dma("gpsimd", "ld", lambda g, tt=tt: g.dma_start(out=ksb[:], in_=k_hbm.ap()[tt]))
            p.wait("scalar", "ld")
            p.wait("scalar", "vs")
            p.sop(lambda s: s.activation(out=s1[:], in_=ksb[:], func=AF.Abs))
            p.sop(lambda s: s.activation(out=s1[:], in_=s1[:], func=AF.Exp, scale=-1.0))
            p.sop(lambda s: s.activation(out=s1[:], in_=s1[:], func=AF.Ln, bias=1.0))
            p.sop(lambda s: s.activation(out=s2[:], in_=ptb[:], func=AF.Abs))
            p.sop(lambda s: s.activation(out=s2[:], in_=s2[:], func=AF.Exp, scale=-1.0))
            p.sop(lambda s: s.activation(out=s2[:], in_=s2[:], func=AF.Ln, bias=1.0))
            p.sop(lambda s: s.activation(out=s3[:], in_=ptb[:], func=AF.Relu))
            p.sop(lambda s: s.activation(out=s3[:], in_=s3[:], func=AF.Ln, bias=0.5))
            p.wait("vector", "ss")
            p.vop(lambda v: v.scalar_tensor_tensor(out=s4[:], in0=ksb[:], scalar=0.0, in1=s1[:], op0=ALU.min, op1=ALU.subtract))
            p.vop(lambda v: v.scalar_tensor_tensor(out=xb0[:], in0=ksb[:], scalar=0.0, in1=s1[:], op0=ALU.max, op1=ALU.add))
            p.vop(lambda v: v.scalar_tensor_tensor(out=s2[:], in0=ptb[:], scalar=0.0, in1=s2[:], op0=ALU.min, op1=ALU.subtract))
            p.vop(lambda v: v.tensor_single_scalar(out=mask_t[:], in_=ptb[:], scalar=0.0, op=ALU.is_ge))
            p.vop(lambda v: v.select(out=s2[:], mask=mask_t[:], on_true=s3[:], on_false=s2[:]))
            p.vop(lambda v: v.tensor_add(out=s4[:], in0=s4[:], in1=s2[:]))
            p.vop(lambda v: v.tensor_tensor_scan(out=s2[:], data0=xb0[:], data1=xb0[:], initial=0.0, op0=ALU.add, op1=ALU.bypass))
            p.vop(lambda v: v.tensor_add(out=s3[:], in0=s4[:], in1=s2[:]))
            p.wait("sync", "vs")
            p.dma("sync", "st", lambda sy, tt=tt: sy.dma_start(out=cumP_r[tt], in_=s2[:]))
            p.dma("sync", "st", lambda sy, tt=tt: sy.dma_start(out=vpr_r[tt], in_=s3[:]))

        w2T_r = w2T.rearrange("(hi lo) d -> lo hi d", lo=128)
        w2pp_r = [w2pp_hbm.ap()[dch].rearrange("(hi l) n -> l hi n", l=128) for dch in range(NCH)]
        p.wait("gpsimd", "vs")
        p.wait("gpsimd", "ss")
        for lo in range(128):
            p.wait("gpsimd", "ts")
            p.dma("gpsimd", "ld", lambda g, lo=lo: g.dma_start(out=s1[0:64, :], in_=w2T_r[lo]))
            p.wait("tensor", "ld")
            p.wait("vector", "st")
            for dch in range(NCH):
                p.wait("tensor", "vs")
                p.op("tensor", "ts", lambda te, dch=dch: te.matmul(pF[:], h64f[:], s1[0:64, dch * 512:(dch + 1) * 512], start=True, stop=True))
                p.wait("vector", "ts")
                p.vop(lambda v, dch=dch: v.tensor_copy(out=xnb[0:64, dch * 512:(dch + 1) * 512], in_=pF[:]))
            p.wait("sync", "vs")
            for dch in range(NCH):
                p.dma("sync", "st", lambda sy, lo=lo, dch=dch: sy.dma_start(out=w2pp_r[dch][lo], in_=xnb[0:64, dch * 512:(dch + 1) * 512]))

        ts_after_gemm = {}
        for hch in range(HCHN):
            par = hch % 2
            base = par * 2 * KT * 128
            sem = "lwa" if par == 0 else "lwb"
            if hch >= 2:
                p.wait("gpsimd", "ts", ts_after_gemm[hch - 2])
            else:
                p.wait("gpsimd", "ts")
            for kt in range(KT):
                p.dma2("gpsimd", sem, lambda g, hch=hch, kt=kt, base=base: g.dma_start(out=wmat[:, base + kt * 128: base + (kt + 1) * 128], in_=w1Tb[hch, kt]))
                p.dma2("gpsimd", sem, lambda g, hch=hch, kt=kt, base=base: g.dma_start(out=wmat[:, base + KT * 128 + kt * 128: base + KT * 128 + (kt + 1) * 128], in_=w3Tb[hch, kt]))
            p.wait("tensor", sem)
            for tokch in range(NCH):
                p.wait("tensor", "vs")
                p.wait("tensor", "ss")
                p.op("tensor", "ts", gemm16(
                    pG[0],
                    lambda kt, base=base: wmat[:, base + kt * 128: base + (kt + 1) * 128],
                    lambda kt, tokch=tokch: xn0_fm[:, kt * MTOK + tokch * 512: kt * MTOK + (tokch + 1) * 512]))
                p.op("tensor", "ts", gemm16(
                    pG[1],
                    lambda kt, base=base: wmat[:, base + KT * 128 + kt * 128: base + KT * 128 + (kt + 1) * 128],
                    lambda kt, tokch=tokch: xn0_fm[:, kt * MTOK + tokch * 512: kt * MTOK + (tokch + 1) * 512]))
                p.wait("scalar", "ts")
                p.wait("scalar", "st")
                p.sop(lambda s: s.activation(out=sgb[:], in_=pG[0][:], func=AF.Silu))
                p.wait("vector", "ts")
                p.vop(lambda v: v.tensor_copy(out=ulb[:], in_=pG[1][:]))
                p.wait("vector", "ss")
                p.vop(lambda v: v.tensor_mul(out=hidb[:], in0=sgb[:], in1=ulb[:]))
                p.wait("tensor", "vs")
                p.op("tensor", "ts", lambda te: te.matmul(pG[2][:], h128b[:], hidb[:], start=True, stop=True))
                p.wait("vector", "ts")
                p.wait("vector", "st")
                p.vop(lambda v: v.tensor_copy(out=hidAb[:], in_=pG[2][:]))
                p.wait("sync", "vs")
                for i in range(4):
                    p.dma("sync", "st", lambda sy, hch=hch, tokch=tokch, i=i: sy.dma_start(
                        out=hidA_hbm.ap()[tokch * 4 + i, hch],
                        in_=hidAb[:, i * 128:(i + 1) * 128]))
            ts_after_gemm[hch] = p.cnt["ts"]

        w2ppb = wmat
        hidAl = xn0_fm
        ts_after_a4 = {}
        for dch in range(NCH):
            p.wait("gpsimd", "ts")
            p.wait("gpsimd", "st")
            for h in range(HCHN):
                p.dma2("gpsimd", "lp", lambda g, dch=dch, h=h: g.dma_start(
                    out=w2ppb[:, h * 512:(h + 1) * 512],
                    in_=w2pp_hbm.ap()[dch, h * 128:(h + 1) * 128, :]))
            lp_cnt = p.cnt["lp"]
            for tt in range(TT):
                par = tt % 2
                hbase = par * HCHN * 128
                hsem = "lha" if par == 0 else "lhb"
                key = (dch, tt)
                prev = (dch, tt - 2) if tt >= 2 else ((dch - 1, TT - 2 + tt) if dch >= 1 else None)
                if prev is not None and prev in ts_after_a4:
                    p.wait("gpsimd", "ts", ts_after_a4[prev])
                for h in range(HCHN):
                    p.dma2("gpsimd", hsem, lambda g, tt=tt, h=h, hbase=hbase: g.dma_start(
                        out=hidAl[:, hbase + h * 128: hbase + (h + 1) * 128],
                        in_=hidA_hbm.ap()[tt, h]))
                p.wait("tensor", hsem)
                p.wait("tensor", "lp", lp_cnt)
                p.wait("tensor", "vs")
                def a4run(te, hbase=hbase):
                    for h in range(HCHN):
                        ins = te.matmul(pG[0][:],
                                        hidAl[:, hbase + h * 128: hbase + (h + 1) * 128],
                                        w2ppb[:, h * 512:(h + 1) * 512],
                                        start=(h == 0), stop=(h == HCHN - 1))
                    return ins
                p.op("tensor", "ts", a4run)
                ts_after_a4[key] = p.cnt["ts"]
                p.wait("vector", "ts")
                p.wait("vector", "st")
                p.vop(lambda v: v.tensor_copy(out=fout[:], in_=pG[0][:]))
                p.wait("sync", "vs")
                p.dma("sync", "st", lambda sy, dch=dch, tt=tt: sy.dma_start(
                    out=ffn0_o[dch].rearrange("(tt p) n -> tt p n", p=128)[tt], in_=fout[:]))

        p.wait("sync", "st")
        p.wait("sync", "ld")
        block = es.enter_context(nc.Block())
        p.attach(block)
    return nc



_CACHE = {}


def _prep_A_host(inputs):
    Wz = inputs["L0_attn_W_z"]
    Wh = inputs["L0_attn_W_h"]
    W1 = inputs["L0_ffn_W1"]
    W3 = inputs["L0_ffn_W3"]
    W2 = inputs["L0_ffn_W2"]
    wzTb = np.ascontiguousarray(Wz.T.reshape(KT, 128, D))
    whTb = np.ascontiguousarray(Wh.T.reshape(KT, 128, D))
    w1Tb = np.ascontiguousarray(W1.T.reshape(KT, 128, HCHN, 128).transpose(2, 0, 1, 3))
    w3Tb = np.ascontiguousarray(W3.T.reshape(KT, 128, HCHN, 128).transpose(2, 0, 1, 3))
    w2T = np.ascontiguousarray(W2.T)
    h64s = (hadamard(64) / np.sqrt(HID)).astype(np.float32)
    h128c = hadamard(128).astype(np.float32)
    iden = np.eye(128, dtype=np.float32)
    bz = inputs["L0_attn_b_z"].reshape(1, D)
    bh = inputs["L0_attn_b_h"].reshape(1, D)
    h0 = inputs["L0_attn_h0"].reshape(1, D)
    x0 = inputs["L0_attn_hidden_states"].reshape(B * T, D)
    maps = []
    for c in range(8):
        maps.append(dict(
            x0s=np.ascontiguousarray(x0[c * MTOK:(c + 1) * MTOK]),
            wzTb=wzTb, whTb=whTb, bzr=bz, bhr=bh, h0r=h0,
            w1Tb=w1Tb, w3Tb=w3Tb, w2T=w2T, h64s=h64s, h128=h128c, iden=iden,
            cst=np.array([[-1.0, 0.5]], np.float32),
        ))
    return maps


def run_A(inputs):
    if "A" not in _CACHE:
        _CACHE["A"] = build_A()
    maps = _prep_A_host(inputs)
    res = run_bass_kernel_spmd(_CACHE["A"], maps, core_ids=list(range(8)))
    return res.results





TP1 = T + 1
HPC = 8


def build_B():
    nc = bass.Bass("TRN2", target_bir_lowering=False)
    vfm = nc.dram_tensor("vfm", [HPC, 128, TP1], F32, kind="ExternalInput").ap()
    cfm = nc.dram_tensor("cfm", [HPC, 128, TP1], F32, kind="ExternalInput").ap()
    onescol = nc.dram_tensor("onescol", [128, 1], F32, kind="ExternalInput").ap()
    logh_o = nc.dram_tensor("logh_o", [HPC, 128, TP1], F32, kind="ExternalOutput").ap()
    part_o = nc.dram_tensor("part_o", [1, 2], F32, kind="ExternalOutput").ap()

    with ExitStack() as es:
        p = Prog(nc, es)
        for s in ("ld", "st", "ts", "vs", "ss"):
            p.sem(s)
        vt = sbuf(nc, es, "vt", [128, TP1], F32)
        ct = sbuf(nc, es, "ct", [128, TP1], F32)
        Mt = sbuf(nc, es, "Mt", [128, TP1], F32)
        At = sbuf(nc, es, "At", [128, TP1], F32)
        Bt = sbuf(nc, es, "Bt", [128, TP1], F32)
        St = sbuf(nc, es, "St", [128, TP1], F32)
        r1 = sbuf(nc, es, "r1", [128, 1], F32)
        pacc = sbuf(nc, es, "pacc", [128, 2], F32)
        onesb = sbuf(nc, es, "onesb", [128, 1], F32)
        stat = sbuf(nc, es, "stat", [1, 2], F32)
        pS = psum(nc, es, "pS", [1, 2], F32)

        p.dma("gpsimd", "ld", lambda g: g.dma_start(out=onesb[:], in_=onescol))
        p.vop(lambda v: v.memset(pacc[:], 0.0))
        for ht in range(HPC):
            p.wait("gpsimd", "vs")
            p.wait("gpsimd", "ss")
            p.wait("gpsimd", "st")
            p.dma("gpsimd", "ld", lambda g, ht=ht: g.dma_start(out=vt[:], in_=vfm[ht]))
            p.dma("gpsimd", "ld", lambda g, ht=ht: g.dma_start(out=ct[:], in_=cfm[ht]))
            p.wait("vector", "ld")
            p.vop(lambda v: v.tensor_tensor_scan(out=Mt[:], data0=vt[:], data1=vt[:], initial=-3e38, op0=ALU.max, op1=ALU.max))
            p.vop(lambda v: v.memset(At[:, 0:1], 0.0))
            p.vop(lambda v: v.tensor_sub(out=At[:, 1:TP1], in0=Mt[:, 0:T], in1=Mt[:, 1:TP1]))
            p.vop(lambda v: v.tensor_sub(out=Bt[:], in0=vt[:], in1=Mt[:]))
            p.wait("scalar", "vs")
            p.sop(lambda s: s.activation(out=At[:], in_=At[:], func=AF.Exp))
            p.sop(lambda s: s.activation(out=Bt[:], in_=Bt[:], func=AF.Exp))
            p.wait("vector", "ss")
            p.vop(lambda v: v.tensor_tensor_scan(out=St[:], data0=At[:], data1=Bt[:], initial=0.0, op0=ALU.mult, op1=ALU.add))
            p.wait("scalar", "vs")
            p.sop(lambda s: s.activation(out=At[:], in_=St[:], func=AF.Ln))
            p.wait("vector", "ss")
            p.vop(lambda v: v.tensor_add(out=Bt[:], in0=At[:], in1=Mt[:]))
            p.vop(lambda v: v.tensor_sub(out=vt[:], in0=Bt[:], in1=ct[:]))
            p.wait("sync", "vs")
            p.dma("sync", "st", lambda sy, ht=ht: sy.dma_start(out=logh_o[ht], in_=vt[:]))
            p.vop(lambda v: v.reduce_sum(out=r1[:], in_=vt[:], axis=AX.X))
            p.vop(lambda v: v.tensor_add(out=pacc[:, 0:1], in0=pacc[:, 0:1], in1=r1[:]))
            p.wait("scalar", "vs")
            p.sop(lambda s: s.activation(out=St[:], in_=vt[:], func=AF.Square, accum_out=r1[:]))
            p.wait("vector", "ss")
            p.vop(lambda v: v.tensor_add(out=pacc[:, 1:2], in0=pacc[:, 1:2], in1=r1[:]))
        p.wait("tensor", "vs")
        p.wait("tensor", "ld")
        p.op("tensor", "ts", lambda te: te.matmul(pS[:], onesb[:], pacc[:], start=True, stop=True))
        p.wait("vector", "ts")
        p.vop(lambda v: v.tensor_copy(out=stat[:], in_=pS[:]))
        p.wait("sync", "vs")
        p.dma("sync", "st", lambda sy: sy.dma_start(out=part_o, in_=stat[:]))
        p.wait("sync", "st")
        block = es.enter_context(nc.Block())
        p.attach(block)
    return nc



def build_C():
    nc = bass.Bass("TRN2", target_bir_lowering=False)
    logh = nc.dram_tensor("logh", [HPC, 128, TP1], F32, kind="ExternalInput").ap()
    parts = nc.dram_tensor("parts", [8, 2], F32, kind="ExternalInput").ap()
    ones8 = nc.dram_tensor("ones8", [8, 1], F32, kind="ExternalInput").ap()
    onesr = nc.dram_tensor("onesr", [1, 128], F32, kind="ExternalInput").ap()
    h_o = nc.dram_tensor("h_o", [HPC, 128, T], F32, kind="ExternalOutput").ap()
    NTOT = float(B * TP1 * D)

    with ExitStack() as es:
        p = Prog(nc, es)
        for s in ("ld", "st", "ts", "vs", "ss"):
            p.sem(s)
        pb = sbuf(nc, es, "pb", [8, 2], F32)
        o8 = sbuf(nc, es, "o8", [8, 1], F32)
        orow = sbuf(nc, es, "orow", [1, 128], F32)
        stat = sbuf(nc, es, "stat", [1, 2], F32)
        sc1 = sbuf(nc, es, "sc1", [1, 2], F32)
        tmp = sbuf(nc, es, "tmp", [1, 1], F32)
        sc2 = sbuf(nc, es, "sc2", [128, 2], F32)
        lt = sbuf(nc, es, "lt", [128, TP1], F32)
        pS = psum(nc, es, "pS", [1, 2], F32)
        pB = psum(nc, es, "pB", [128, 2], F32)

        p.dma("gpsimd", "ld", lambda g: g.dma_start(out=pb[:], in_=parts))
        p.dma("gpsimd", "ld", lambda g: g.dma_start(out=o8[:], in_=ones8))
        p.dma("gpsimd", "ld", lambda g: g.dma_start(out=orow[:], in_=onesr))
        p.wait("tensor", "ld")
        p.op("tensor", "ts", lambda te: te.matmul(pS[:], o8[:], pb[:], start=True, stop=True))
        p.wait("vector", "ts")
        p.vop(lambda v: v.tensor_copy(out=stat[:], in_=pS[:]))
        p.vop(lambda v: v.tensor_scalar(out=sc1[:, 0:1], in0=stat[:, 0:1], scalar1=1.0 / NTOT, scalar2=None, op0=ALU.mult))
        p.vop(lambda v: v.tensor_mul(out=tmp[:], in0=stat[:, 0:1], in1=stat[:, 0:1]))
        p.vop(lambda v: v.tensor_scalar(out=tmp[:], in0=tmp[:], scalar1=1.0 / NTOT, scalar2=None, op0=ALU.mult))
        p.vop(lambda v: v.tensor_sub(out=tmp[:], in0=stat[:, 1:2], in1=tmp[:]))
        p.vop(lambda v: v.tensor_scalar(out=tmp[:], in0=tmp[:], scalar1=1.0 / (NTOT - 1.0), scalar2=None, op0=ALU.mult))
        p.wait("scalar", "vs")
        p.sop(lambda s: s.activation(out=tmp[:], in_=tmp[:], func=AF.Sqrt))
        p.wait("vector", "ss")
        p.vop(lambda v: v.reciprocal(out=sc1[:, 1:2], in_=tmp[:]))
        p.vop(lambda v: v.tensor_mul(out=sc1[:, 0:1], in0=sc1[:, 0:1], in1=sc1[:, 1:2]))
        p.vop(lambda v: v.tensor_scalar(out=sc1[:, 0:1], in0=sc1[:, 0:1], scalar1=-1.0, scalar2=None, op0=ALU.mult))
        p.wait("tensor", "vs")
        p.op("tensor", "ts", lambda te: te.matmul(pB[:], orow[:], sc1[:], start=True, stop=True))
        p.wait("vector", "ts")
        p.vop(lambda v: v.tensor_copy(out=sc2[:], in_=pB[:]))
        sem_sc = p.cnt["vs"]
        for ht in range(HPC):
            p.wait("gpsimd", "ss")
            p.dma("gpsimd", "ld", lambda g, ht=ht: g.dma_start(out=lt[:], in_=logh[ht]))
            p.wait("scalar", "ld")
            p.wait("scalar", "vs", sem_sc)
            p.wait("scalar", "st")
            p.sop(lambda s: s.activation(out=lt[:], in_=lt[:], func=AF.Exp, scale=sc2[:, 1:2], bias=sc2[:, 0:1]))
            p.wait("sync", "ss")
            p.dma("sync", "st", lambda sy, ht=ht: sy.dma_start(out=h_o[ht], in_=lt[:, 1:TP1]))
        p.wait("sync", "st")
        block = es.enter_context(nc.Block())
        p.attach(block)
    return nc


def run_B(vfull, cfull):
    if "B" not in _CACHE:
        _CACHE["B"] = build_B()
    maps = []
    ones = np.ones((128, 1), np.float32)
    for c in range(8):
        b, j = c // 2, c % 2
        hsl = slice(j * 1024, (j + 1) * 1024)
        vfm = np.ascontiguousarray(vfull[b, :, hsl].T).reshape(HPC, 128, TP1)
        cfm = np.ascontiguousarray(cfull[b, :, hsl].T).reshape(HPC, 128, TP1)
        maps.append(dict(vfm=vfm, cfm=cfm, onescol=ones))
    res = run_bass_kernel_spmd(_CACHE["B"], maps, core_ids=list(range(8)))
    return res.results


def run_C(loghs, parts):
    if "C" not in _CACHE:
        _CACHE["C"] = build_C()
    maps = []
    for c in range(8):
        maps.append(dict(logh=loghs[c], parts=parts,
                         ones8=np.ones((8, 1), np.float32),
                         onesr=np.ones((1, 128), np.float32)))
    res = run_bass_kernel_spmd(_CACHE["C"], maps, core_ids=list(range(8)))
    return res.results





def build_D():
    nc = bass.Bass("TRN2", target_bir_lowering=False)
    hs = nc.dram_tensor("hs", [MTOK, D], F32, kind="ExternalInput").ap()
    xn0s = nc.dram_tensor("xn0s", [MTOK, D], F32, kind="ExternalInput").ap()
    x0s = nc.dram_tensor("x0s", [MTOK, D], F32, kind="ExternalInput").ap()
    ffn0s = nc.dram_tensor("ffn0s", [MTOK, D], F32, kind="ExternalInput").ap()
    lnw = nc.dram_tensor("lnw", [1, D], F32, kind="ExternalInput").ap()
    lnb = nc.dram_tensor("lnb", [1, D], F32, kind="ExternalInput").ap()
    nwr = nc.dram_tensor("nwr", [1, D], F32, kind="ExternalInput").ap()
    nwcol = nc.dram_tensor("nwcol", [128, KT], F32, kind="ExternalInput").ap()
    spbr = nc.dram_tensor("spbr", [1, D], F32, kind="ExternalInput").ap()
    sgr = nc.dram_tensor("sgr", [1, D], F32, kind="ExternalInput").ap()
    dtbr = nc.dram_tensor("dtbr", [1, D], F32, kind="ExternalInput").ap()
    spwTb = nc.dram_tensor("spwTb", [KT, 128, D], F32, kind="ExternalInput").ap()
    xpwTb = nc.dram_tensor("xpwTb", [KT, 128, 384], F32, kind="ExternalInput").ap()
    dtwT = nc.dram_tensor("dtwT", [128, D], F32, kind="ExternalInput").ap()
    iden = nc.dram_tensor("iden", [128, 128], F32, kind="ExternalInput").ap()
    cst = nc.dram_tensor("cst", [1, 2], F32, kind="ExternalInput").ap()

    x1_o = nc.dram_tensor("x1_o", [MTOK, D], F32, kind="ExternalOutput").ap()
    xn1_o = nc.dram_tensor("xn1_o", [MTOK, D], F32, kind="ExternalOutput").ap()
    delta_o = nc.dram_tensor("delta_o", [MTOK, D], F32, kind="ExternalOutput").ap()
    dx_o = nc.dram_tensor("dx_o", [MTOK, D], F32, kind="ExternalOutput").ap()
    Bssm_o = nc.dram_tensor("Bssm_o", [MTOK, NS], F32, kind="ExternalOutput").ap()
    Cssm_o = nc.dram_tensor("Cssm_o", [MTOK, NS], F32, kind="ExternalOutput").ap()

    hs_r = hs.rearrange("(tt p) d -> tt p d", p=128)
    xn0_r = xn0s.rearrange("(tt p) d -> tt p d", p=128)
    x0_r = x0s.rearrange("(tt p) d -> tt p d", p=128)
    ffn_r = ffn0s.rearrange("(tt p) d -> tt p d", p=128)
    x1_r = x1_o.rearrange("(tt p) d -> tt p d", p=128)
    xn1_r = xn1_o.rearrange("(tt p) d -> tt p d", p=128)
    del_r = delta_o.rearrange("(tt p) d -> tt p d", p=128)
    dx_r = dx_o.rearrange("(tt p) d -> tt p d", p=128)
    Bs_r = Bssm_o.rearrange("(tt p) d -> tt p d", p=128)
    Cs_r = Cssm_o.rearrange("(tt p) d -> tt p d", p=128)

    with ExitStack() as es:
        p = Prog(nc, es)
        for s in ("ld", "st", "ts", "vs", "ss"):
            p.sem(s)

        spw = sbuf(nc, es, "spw", [128, KT * D], BF16)
        xpw = sbuf(nc, es, "xpw", [128, KT * 384], BF16)
        dtw = sbuf(nc, es, "dtw", [128, D], BF16)
        lnwb = sbuf(nc, es, "lnwb", [128, D], BF16)
        lnbb = sbuf(nc, es, "lnbb", [128, D], BF16)
        nwb = sbuf(nc, es, "nwb", [128, D], BF16)
        nwc = sbuf(nc, es, "nwc", [128, KT], F32)
        spbb = sbuf(nc, es, "spbb", [128, D], BF16)
        sgbb = sbuf(nc, es, "sgbb", [128, D], BF16)
        dtbb = sbuf(nc, es, "dtbb", [128, D], BF16)
        idb = sbuf(nc, es, "idb", [128, 128], BF16)
        tA = sbuf(nc, es, "tA", [128, D], F32)
        tB = sbuf(nc, es, "tB", [128, D], F32)
        sA = sbuf(nc, es, "sA", [128, D], F32)
        sB = sbuf(nc, es, "sB", [128, D], F32)
        sC = sbuf(nc, es, "sC", [128, D], F32)
        sD = sbuf(nc, es, "sD", [128, D], F32)
        sG = sbuf(nc, es, "sG", [128, D], F32)
        xbb = sbuf(nc, es, "xbb", [128, D], BF16)
        fmb = sbuf(nc, es, "fmb", [128, KT * 128], BF16)
        fmb2 = sbuf(nc, es, "fmb2", [128, KT * 128], BF16)
        rkb = sbuf(nc, es, "rkb", [128, 128], BF16)
        sE = sbuf(nc, es, "sE", [128, 384], F32)
        mask_t = sbuf(nc, es, "maskd", [128, D], U8)
        red = sbuf(nc, es, "redd", [128, 1], F32)
        red2 = sbuf(nc, es, "redd2", [128, 1], F32)
        rrms = sbuf(nc, es, "rrms", [128, 1], F32)

        pG = [psum(nc, es, f"pGD{i}", [128, 512], F32) for i in range(3)]
        pTr = psum(nc, es, "pTrD", [128, 128], BF16)
        pX = psum(nc, es, "pXD", [128, 384], F32)

        reg_act_consts(nc, es, p, cst, [-1.0, 0.5])
        p.dma("gpsimd", "ld", lambda g: g.dma_start(out=lnwb[:], in_=lnw.partition_broadcast(128)))
        p.dma("gpsimd", "ld", lambda g: g.dma_start(out=lnbb[:], in_=lnb.partition_broadcast(128)))
        p.dma("gpsimd", "ld", lambda g: g.dma_start(out=nwb[:], in_=nwr.partition_broadcast(128)))
        p.dma("gpsimd", "ld", lambda g: g.dma_start(out=nwc[:], in_=nwcol))
        p.dma("gpsimd", "ld", lambda g: g.dma_start(out=spbb[:], in_=spbr.partition_broadcast(128)))
        p.dma("gpsimd", "ld", lambda g: g.dma_start(out=sgbb[:], in_=sgr.partition_broadcast(128)))
        p.dma("gpsimd", "ld", lambda g: g.dma_start(out=dtbb[:], in_=dtbr.partition_broadcast(128)))
        p.dma("gpsimd", "ld", lambda g: g.dma_start(out=idb[:], in_=iden))
        p.dma("gpsimd", "ld", lambda g: g.dma_start(out=dtw[:], in_=dtwT))
        for kt in range(KT):
            p.dma("gpsimd", "ld", lambda g, kt=kt: g.dma_start(out=xpw[:, kt * 384:(kt + 1) * 384], in_=xpwTb[kt]))
        for kt in range(KT):
            p.dma("gpsimd", "ld", lambda g, kt=kt: g.dma_start(out=spw[:, kt * D:(kt + 1) * D], in_=spwTb[kt]))
        p.wait("vector", "ld")
        for kt in range(KT):
            p.vop(lambda v, kt=kt: v.tensor_scalar(out=spw[:, kt * D:(kt + 1) * D], in0=spw[:, kt * D:(kt + 1) * D], scalar1=nwc[:, kt:kt + 1], scalar2=None, op0=ALU.mult))

        def ln_chain(src, dst, eps):
            p.vop(lambda v: v.reduce_sum(out=red[:], in_=src[:], axis=AX.X))
            p.vop(lambda v: v.tensor_scalar(out=red[:], in0=red[:], scalar1=1.0 / D, scalar2=None, op0=ALU.mult))
            p.vop(lambda v: v.tensor_scalar(out=dst[:], in0=src[:], scalar1=red[:], scalar2=None, op0=ALU.subtract))
            p.wait("scalar", "vs")
            p.sop(lambda s: s.activation(out=tB[:], in_=dst[:], func=AF.Square, accum_out=red2[:]))
            p.wait("vector", "ss")
            p.vop(lambda v: v.tensor_scalar(out=red2[:], in0=red2[:], scalar1=1.0 / D, scalar2=eps, op0=ALU.mult, op1=ALU.add))
            p.wait("scalar", "vs")
            p.sop(lambda s: s.activation(out=red2[:], in_=red2[:], func=AF.Sqrt))
            p.wait("vector", "ss")
            p.vop(lambda v: v.reciprocal(out=red2[:], in_=red2[:]))
            p.vop(lambda v: v.tensor_scalar(out=dst[:], in0=dst[:], scalar1=red2[:], scalar2=None, op0=ALU.mult))

        def transpose16(srcb, dstfm):
            p.wait("tensor", "vs")
            for kt in range(KT):
                p.op("tensor", "ts", lambda te, kt=kt: te.transpose(pTr[:], srcb[:, kt * 128:(kt + 1) * 128], idb[:]))
                p.wait("vector", "ts")
                p.vop(lambda v, kt=kt: v.tensor_copy(out=dstfm[:, kt * 128:(kt + 1) * 128], in_=pTr[:]))
                p.wait("tensor", "vs")

        for tt in range(TT):
            p.wait("gpsimd", "vs")
            p.wait("gpsimd", "ss")
            p.dma("gpsimd", "ld", lambda g, tt=tt: g.dma_start(out=tA[:], in_=hs_r[tt]))
            p.dma("gpsimd", "ld", lambda g, tt=tt: g.dma_start(out=sA[:], in_=xn0_r[tt]))
            p.wait("vector", "ld")
            p.wait("vector", "st")
            p.vop(lambda v: v.tensor_add(out=sA[:], in0=sA[:], in1=tA[:]))
            ln_chain(sA, sB, LN_EPS)
            p.vop(lambda v: v.tensor_tensor(out=sB[:], in0=sB[:], in1=lnwb[:], op=ALU.mult))
            p.op("vector", "vs", lambda v: v.tensor_tensor(out=sB[:], in0=sB[:], in1=lnbb[:], op=ALU.add))
            p.wait("gpsimd", "vs")
            p.dma("gpsimd", "ld", lambda g, tt=tt: g.dma_start(out=tA[:], in_=x0_r[tt]))
            p.dma("gpsimd", "ld", lambda g, tt=tt: g.dma_start(out=sC[:], in_=ffn_r[tt]))
            p.wait("vector", "ld")
            p.vop(lambda v: v.tensor_add(out=sA[:], in0=sB[:], in1=tA[:]))
            p.vop(lambda v: v.tensor_add(out=sA[:], in0=sA[:], in1=sC[:]))
            p.wait("sync", "vs")
            p.dma("sync", "st", lambda sy, tt=tt: sy.dma_start(out=x1_r[tt], in_=sA[:]))
            ln_chain(sA, sB, LN_EPS)
            p.op("vector", "vs", lambda v: v.engine_nop())
            p.wait("sync", "vs")
            p.dma("sync", "st", lambda sy, tt=tt: sy.dma_start(out=xn1_r[tt], in_=sB[:]))
            p.wait("scalar", "vs")
            p.sop(lambda s: s.activation(out=tB[:], in_=sB[:], func=AF.Square, accum_out=rrms[:]))
            p.wait("vector", "ss")
            p.vop(lambda v: v.tensor_scalar(out=rrms[:], in0=rrms[:], scalar1=1.0 / D, scalar2=1e-6, op0=ALU.mult, op1=ALU.add))
            p.wait("scalar", "vs")
            p.sop(lambda s: s.activation(out=rrms[:], in_=rrms[:], func=AF.Sqrt))
            p.wait("vector", "ss")
            p.vop(lambda v: v.reciprocal(out=rrms[:], in_=rrms[:]))
            p.vop(lambda v: v.tensor_scalar(out=sC[:], in0=sB[:], scalar1=rrms[:], scalar2=None, op0=ALU.mult))
            p.vop(lambda v: v.tensor_tensor(out=sC[:], in0=sC[:], in1=nwb[:], op=ALU.mult))
            p.vop(lambda v: v.tensor_copy(out=xbb[:], in_=sB[:]))
            transpose16(xbb, fmb)
            for nch in range(NCH):
                bank = pG[nch % 3]
                p.wait("tensor", "vs")
                p.op("tensor", "ts", gemm16(
                    bank,
                    lambda kt: fmb[:, kt * 128:(kt + 1) * 128],
                    lambda kt, nch=nch: spw[:, kt * D + nch * 512: kt * D + (nch + 1) * 512]))
                p.wait("vector", "ts")
                p.vop(lambda v, bank=bank, nch=nch: v.tensor_scalar(
                    out=sD[:, nch * 512:(nch + 1) * 512], in0=bank[:], scalar1=rrms[:], scalar2=None, op0=ALU.mult))
            p.vop(lambda v: v.tensor_tensor(out=sD[:], in0=sD[:], in1=spbb[:], op=ALU.add))
            p.op("vector", "vs", lambda v: v.tensor_tensor(out=sD[:], in0=sD[:], in1=sgbb[:], op=ALU.mult))
            p.wait("scalar", "vs")
            p.sop(lambda s: s.activation(out=sD[:], in_=sD[:], func=AF.Sigmoid))
            p.wait("vector", "ss")
            p.vop(lambda v: v.tensor_tensor(out=sC[:], in0=sC[:], in1=sD[:], op=ALU.mult))
            p.vop(lambda v: v.tensor_copy(out=xbb[:], in_=sC[:]))
            transpose16(xbb, fmb2)
            p.wait("tensor", "vs")
            p.op("tensor", "ts", lambda te: [te.matmul(pX[:], fmb2[:, kt * 128:(kt + 1) * 128],
                                                       xpw[:, kt * 384:(kt + 1) * 384],
                                                       start=(kt == 0), stop=(kt == KT - 1))
                                             for kt in range(KT)][-1])
            p.wait("vector", "ts")
            p.op("vector", "vs", lambda v: v.tensor_copy(out=sE[:], in_=pX[:]))
            p.wait("sync", "vs")
            p.dma("sync", "st", lambda sy, tt=tt: sy.dma_start(out=Bs_r[tt], in_=sE[:, 128:256]))
            p.dma("sync", "st", lambda sy, tt=tt: sy.dma_start(out=Cs_r[tt], in_=sE[:, 256:384]))
            p.vop(lambda v: v.tensor_copy(out=xbb[:, 0:128], in_=sE[:, 0:128]))
            p.wait("tensor", "vs")
            p.op("tensor", "ts", lambda te: te.transpose(pTr[:], xbb[:, 0:128], idb[:]))
            p.wait("vector", "ts")
            p.op("vector", "vs", lambda v: v.tensor_copy(out=rkb[:], in_=pTr[:]))
            p.wait("tensor", "vs")
            for nch in range(NCH):
                bank = pG[nch % 3]
                p.wait("tensor", "vs")
                p.op("tensor", "ts", lambda te, bank=bank, nch=nch: te.matmul(
                    bank[:], rkb[:], dtw[:, nch * 512:(nch + 1) * 512], start=True, stop=True))
                p.wait("vector", "ts")
                p.vop(lambda v, bank=bank, nch=nch: v.tensor_tensor(
                    out=sG[:, nch * 512:(nch + 1) * 512], in0=bank[:],
                    in1=dtbb[:, nch * 512:(nch + 1) * 512], op=ALU.add))
            p.op("vector", "vs", lambda v: v.engine_nop())
            p.wait("scalar", "vs")
            p.sop(lambda s: s.activation(out=tB[:], in_=sG[:], func=AF.Abs))
            p.sop(lambda s: s.activation(out=tB[:], in_=tB[:], func=AF.Exp, scale=-1.0))
            p.sop(lambda s: s.activation(out=tB[:], in_=tB[:], func=AF.Ln, bias=1.0))
            p.wait("vector", "ss")
            p.vop(lambda v: v.scalar_tensor_tensor(out=sG[:], in0=sG[:], scalar=0.0, in1=tB[:], op0=ALU.max, op1=ALU.add))
            p.op("vector", "vs", lambda v: v.tensor_mul(out=sD[:], in0=sG[:], in1=sC[:]))
            p.wait("sync", "vs")
            p.dma("sync", "st", lambda sy, tt=tt: sy.dma_start(out=del_r[tt], in_=sG[:]))
            p.dma("sync", "st", lambda sy, tt=tt: sy.dma_start(out=dx_r[tt], in_=sD[:]))

        p.wait("sync", "st")
        block = es.enter_context(nc.Block())
        p.attach(block)
    return nc


def run_D(inputs, h_tm, xn0, x0, ffn0):
    if "D" not in _CACHE:
        _CACHE["D"] = build_D()
    nw = inputs["L1_attn_norm_w"].reshape(1, D)
    maps = []
    spwTb = np.ascontiguousarray(inputs["L1_attn_select_proj_w"].T.reshape(KT, 128, D))
    xpwTb = np.ascontiguousarray(inputs["L1_attn_x_proj_w"].T.reshape(KT, 128, 384))
    dtwT = np.ascontiguousarray(inputs["L1_attn_dt_proj_w"].T)
    nwcol = np.ascontiguousarray(nw.reshape(KT, 128).T)
    for c in range(8):
        sl = slice(c * MTOK, (c + 1) * MTOK)
        maps.append(dict(
            hs=np.ascontiguousarray(h_tm[sl]), xn0s=np.ascontiguousarray(xn0[sl]),
            x0s=np.ascontiguousarray(x0[sl]), ffn0s=np.ascontiguousarray(ffn0[sl]),
            lnw=inputs["L0_attn_ln_w"].reshape(1, D), lnb=inputs["L0_attn_ln_b"].reshape(1, D),
            nwr=nw, nwcol=nwcol,
            spbr=inputs["L1_attn_select_proj_b"].reshape(1, D),
            sgr=inputs["L1_attn_select_gate"].reshape(1, D),
            dtbr=inputs["L1_attn_dt_proj_b"].reshape(1, D),
            spwTb=spwTb, xpwTb=xpwTb, dtwT=dtwT,
            iden=np.eye(128, dtype=np.float32),
            cst=np.array([[-1.0, 0.5]], np.float32),
        ))
    res = run_bass_kernel_spmd(_CACHE["D"], maps, core_ids=list(range(8)))
    return res.results





DTC = 8
TCH = 1024
NTC = T // TCH


def build_E():
    nc = bass.Bass("TRN2", target_bir_lowering=False)
    dfm = nc.dram_tensor("dfm", [DTC, 128, T], F32, kind="ExternalInput").ap()
    xfm = nc.dram_tensor("xfm", [DTC, 128, T], F32, kind="ExternalInput").ap()
    BTb = nc.dram_tensor("BTb", [NS, T], F32, kind="ExternalInput").ap()
    CTb = nc.dram_tensor("CTb", [NS, T], F32, kind="ExternalInput").ap()
    Alg = nc.dram_tensor("Alg", [DTC, 128, NS], F32, kind="ExternalInput").ap()
    y_o = nc.dram_tensor("y_o", [DTC, 128, T], F32, kind="ExternalOutput").ap()
    GD = 4

    with ExitStack() as es:
        p = Prog(nc, es)
        for s in ("ld", "st", "ts", "vs", "ss", "lb0", "lb1"):
            p.sem(s)
        At = sbuf(nc, es, "At", [128, DTC * NS], F32)
        dC = sbuf(nc, es, "dC", [128, GD * T], BF16)
        xC = sbuf(nc, es, "xC", [128, GD * T], BF16)
        yak = sbuf(nc, es, "yak", [128, GD * T], F32)
        Bbb = [sbuf(nc, es, f"Bbb{i}", [128, T], BF16) for i in range(2)]
        Cbb = [sbuf(nc, es, f"Cbb{i}", [128, T], BF16) for i in range(2)]
        ab = [sbuf(nc, es, f"ab{i}", [128, T], BF16) for i in range(2)]
        ub = sbuf(nc, es, "ub", [128, T], BF16)
        hb = sbuf(nc, es, "hb", [128, T], BF16)
        zb = sbuf(nc, es, "zb", [128, T], BF16)

        for dt in range(DTC):
            p.dma("gpsimd", "ld", lambda g, dt=dt: g.dma_start(out=At[:, dt * NS:(dt + 1) * NS], in_=Alg[dt]))
        p.wait("scalar", "ld")
        p.sop(lambda s: s.activation(out=At[:], in_=At[:], func=AF.Exp))
        p.wait("vector", "ss")
        p.vop(lambda v: v.tensor_scalar(out=At[:], in0=At[:], scalar1=-1.0, scalar2=None, op0=ALU.mult))
        p.wait("scalar", "vs")

        ab_last_vs = [0, 0]
        bc_used_vs = [0, 0]
        for grp in range(DTC // GD):
            p.wait("gpsimd", "vs")
            p.wait("gpsimd", "ss")
            p.wait("gpsimd", "st")
            for i in range(GD):
                dt = grp * GD + i
                p.dma("gpsimd", "ld", lambda g, dt=dt, i=i: g.dma_start(out=dC[:, i * T:(i + 1) * T], in_=dfm[dt]))
                p.dma("gpsimd", "ld", lambda g, dt=dt, i=i: g.dma_start(out=xC[:, i * T:(i + 1) * T], in_=xfm[dt]))
            p.wait("scalar", "ld")
            p.wait("vector", "st")
            for n in range(NS):
                par = n % 2
                p.wait("gpsimd", "vs", bc_used_vs[par])
                lbs = "lb0" if par == 0 else "lb1"
                p.dma2("gpsimd", lbs, lambda g, n=n, par=par: g.dma_start(out=Bbb[par][:], in_=BTb[n:n + 1, :].partition_broadcast(128)))
                p.dma2("gpsimd", lbs, lambda g, n=n, par=par: g.dma_start(out=Cbb[par][:], in_=CTb[n:n + 1, :].partition_broadcast(128)))
                p.wait("vector", lbs)
                for i in range(GD):
                    dt = grp * GD + i
                    apar = (n * GD + i) % 2
                    col = dt * NS + n
                    p.wait("scalar", "vs", ab_last_vs[apar])
                    p.op("scalar", "ss", lambda s, i=i, col=col, apar=apar: s.activation(
                        out=ab[apar][:], in_=dC[:, i * T:(i + 1) * T], func=AF.Exp,
                        scale=At[:, col:col + 1]))
                    p.wait("vector", "ss")
                    p.vop(lambda v, i=i, par=par: v.tensor_mul(
                        out=ub[:], in0=xC[:, i * T:(i + 1) * T], in1=Bbb[par][:]))
                    p.vop(lambda v, apar=apar: v.tensor_tensor_scan(
                        out=hb[:], data0=ab[apar][:], data1=ub[:],
                        initial=0.0, op0=ALU.mult, op1=ALU.add))
                    ab_last_vs[apar] = p.cnt["vs"]
                    p.vop(lambda v, par=par: v.tensor_mul(out=zb[:], in0=hb[:], in1=Cbb[par][:]))
                    if n == 0:
                        p.vop(lambda v, i=i: v.tensor_copy(out=yak[:, i * T:(i + 1) * T], in_=zb[:]))
                    else:
                        p.vop(lambda v, i=i: v.tensor_add(
                            out=yak[:, i * T:(i + 1) * T],
                            in0=yak[:, i * T:(i + 1) * T], in1=zb[:]))
                bc_used_vs[par] = p.cnt["vs"]
            p.wait("sync", "vs")
            for i in range(GD):
                dt = grp * GD + i
                p.dma("sync", "st", lambda sy, dt=dt, i=i: sy.dma_start(out=y_o[dt], in_=yak[:, i * T:(i + 1) * T]))
        p.wait("sync", "st")
        block = es.enter_context(nc.Block())
        p.attach(block)
    return nc


def run_E(inputs, delta_tm, dx_tm, Bssm, Cssm):
    if "E" not in _CACHE:
        _CACHE["E"] = build_E()
    Alog = inputs["L1_attn_A_log"]
    d4 = delta_tm.reshape(B, T, D)
    x4 = dx_tm.reshape(B, T, D)
    B4 = Bssm.reshape(B, T, NS)
    C4 = Cssm.reshape(B, T, NS)
    maps = []
    for c in range(8):
        b, j = c // 2, c % 2
        dsl = slice(j * 1024, (j + 1) * 1024)
        maps.append(dict(
            dfm=np.ascontiguousarray(d4[b].T[dsl]).reshape(DTC, 128, T),
            xfm=np.ascontiguousarray(x4[b].T[dsl]).reshape(DTC, 128, T),
            BTb=np.ascontiguousarray(B4[b].T),
            CTb=np.ascontiguousarray(C4[b].T),
            Alg=np.ascontiguousarray(Alog[dsl]).reshape(DTC, 128, NS),
        ))
    res = run_bass_kernel_spmd(_CACHE["E"], maps, core_ids=list(range(8)))
    return res.results





def build_F():
    nc = bass.Bass("TRN2", target_bir_lowering=False)
    ys = nc.dram_tensor("ys", [MTOK, D], F32, kind="ExternalInput").ap()
    xn1s = nc.dram_tensor("xn1s", [MTOK, D], F32, kind="ExternalInput").ap()
    x1s = nc.dram_tensor("x1s", [MTOK, D], F32, kind="ExternalInput").ap()
    dpr = nc.dram_tensor("dpr", [1, D], F32, kind="ExternalInput").ap()
    opwTb = nc.dram_tensor("opwTb", [KT, 128, D], F32, kind="ExternalInput").ap()
    wrb = nc.dram_tensor("wrb", [KT, 128, E], F32, kind="ExternalInput").ap()
    we1b = nc.dram_tensor("we1b", [E, KT, 128, HE], F32, kind="ExternalInput").ap()
    we3b = nc.dram_tensor("we3b", [E, KT, 128, HE], F32, kind="ExternalInput").ap()
    we2b = nc.dram_tensor("we2b", [E, NCH, 8, 128, 512], F32, kind="ExternalInput").ap()
    ws1b = nc.dram_tensor("ws1b", [8, KT, 128, 512], F32, kind="ExternalInput").ap()
    ws3b = nc.dram_tensor("ws3b", [8, KT, 128, 512], F32, kind="ExternalInput").ap()
    ws2b = nc.dram_tensor("ws2b", [NCH, 32, 128, 512], F32, kind="ExternalInput").ap()
    iden = nc.dram_tensor("iden", [128, 128], F32, kind="ExternalInput").ap()
    out_o = nc.dram_tensor("out_o", [MTOK, D], F32, kind="ExternalOutput").ap()
    at1_hbm = nc.dram_tensor("at1_hbm", [TT, 128, D], F32)
    moe_hbm = nc.dram_tensor("moe_hbm", [TT, 128, D], F32)

    ys_r = ys.rearrange("(tt p) d -> tt p d", p=128)
    xn1_r = xn1s.rearrange("(tt p) d -> tt p d", p=128)
    x1_r = x1s.rearrange("(tt p) d -> tt p d", p=128)
    out_r = out_o.rearrange("(tt p) d -> tt p d", p=128)

    with ExitStack() as es:
        p = Prog(nc, es)
        for s in ("lw", "lx", "st", "ts", "vs", "ss"):
            p.sem(s)
        wbig = sbuf(nc, es, "wbig", [128, 2 * KT * HE], BF16)
        U1 = sbuf(nc, es, "U1", [128, 8 * D], F32)
        xnh = sbuf(nc, es, "xnh", [128, KT * 1024], BF16)
        we2s = sbuf(nc, es, "we2s", [128, 8 * 512], BF16)
        sY = sbuf(nc, es, "sY", [128, 512], F32)
        gbuf = sbuf(nc, es, "gbuf", [128, 8 * 512], BF16)
        wrt = sbuf(nc, es, "wrt", [128, KT * E], BF16)
        dpb = sbuf(nc, es, "dpb", [128, D], BF16)
        idb = sbuf(nc, es, "idb", [128, 128], BF16)
        sA = sbuf(nc, es, "sA", [128, 512], F32)
        ybb = sbuf(nc, es, "ybb", [128, D], BF16)
        xbb = sbuf(nc, es, "xbb", [128, D], BF16)
        fmb = sbuf(nc, es, "fmb", [128, KT * 128], BF16)
        sgb = sbuf(nc, es, "sgb", [128, 512], BF16)
        shb = sbuf(nc, es, "shb", [128, 512], BF16)
        mo = sbuf(nc, es, "mo", [128, 512], F32)
        sX = sbuf(nc, es, "sX", [128, 512], F32)
        cw_all = sbuf(nc, es, "cw_all", [128, TT * E], F32)
        prb = sbuf(nc, es, "prb", [128, E], F32)
        pwk = sbuf(nc, es, "pwk", [128, E], F32)
        msk = sbuf(nc, es, "msk", [128, E], U8)
        ninf = sbuf(nc, es, "ninf", [128, E], F32)
        zer = sbuf(nc, es, "zer", [128, E], F32)
        red = sbuf(nc, es, "redf", [128, 1], F32)
        pG = [psum(nc, es, f"pGF{i}", [128, 512], F32) for i in range(6)]
        pTr = [psum(nc, es, f"pTrF{i}", [128, 128], BF16) for i in range(2)]

        p.dma("gpsimd", "lx", lambda g: g.dma_start(out=dpb[:], in_=dpr.partition_broadcast(128)))
        p.dma("gpsimd", "lx", lambda g: g.dma_start(out=idb[:], in_=iden))
        for kt in range(KT):
            p.dma("gpsimd", "lx", lambda g, kt=kt: g.dma_start(out=wrt[:, kt * E:(kt + 1) * E], in_=wrb[kt]))
            p.dma2("gpsimd", "lw", lambda g, kt=kt: g.dma_start(out=wbig[:, kt * D:(kt + 1) * D], in_=opwTb[kt]))
        p.vop(lambda v: v.memset(ninf[:], -3e38))
        p.vop(lambda v: v.memset(zer[:], 0.0))

        def transpose16(srcb, dstfm, colf, ntile=KT):
            p.wait("tensor", "vs")
            for kt in range(ntile):
                b = pTr[kt % 2]
                p.op("tensor", "ts", lambda te, kt=kt, b=b: te.transpose(b[:], srcb[:, kt * 128:(kt + 1) * 128], idb[:]))
                p.wait("vector", "ts")
                p.vop(lambda v, kt=kt, b=b: v.tensor_copy(out=dstfm[:, colf(kt):colf(kt) + 128], in_=b[:]))
                if kt + 1 < ntile:
                    p.wait("tensor", "vs", p.cnt["vs"] - 1)

        p.wait("tensor", "lw")
        p.wait("tensor", "lx")
        for tt in range(TT):
            p.wait("gpsimd", "vs")
            p.wait("gpsimd", "st")
            p.wait("gpsimd", "ts")
            p.dma("gpsimd", "lx", lambda g, tt=tt: g.dma_start(out=ybb[:], in_=ys_r[tt]))
            p.dma("gpsimd", "lx", lambda g, tt=tt: g.dma_start(out=xbb[:], in_=xn1_r[tt]))
            p.wait("vector", "lx")
            p.wait("vector", "st")
            p.wait("tensor", "lx")
            transpose16(xbb, fmb, lambda kt: kt * 128)
            p.wait("tensor", "vs")
            def router_mm(te, tt=tt):
                for kt in range(KT):
                    ins = te.matmul(pG[2][:, 0:E],
                                    fmb[:, kt * 128:(kt + 1) * 128],
                                    wrt[:, kt * E:(kt + 1) * E],
                                    start=(kt == 0), stop=(kt == KT - 1))
                return ins
            p.op("tensor", "ts", router_mm)
            p.wait("vector", "ts")
            p.vop(lambda v: v.tensor_copy(out=prb[:], in_=pG[2][:, 0:E]))
            p.vop(lambda v: v.reduce_max(out=red[:], in_=prb[:], axis=AX.X))
            p.vop(lambda v: v.tensor_scalar(out=red[:], in0=red[:], scalar1=-1.0, scalar2=None, op0=ALU.mult))
            p.wait("scalar", "vs")
            p.sop(lambda s: s.activation(out=prb[:], in_=prb[:], func=AF.Exp, bias=red[:]))
            p.wait("vector", "ss")
            p.vop(lambda v: v.reduce_sum(out=red[:], in_=prb[:], axis=AX.X))
            p.vop(lambda v: v.reciprocal(out=red[:], in_=red[:]))
            p.vop(lambda v: v.tensor_scalar(out=prb[:], in0=prb[:], scalar1=red[:], scalar2=None, op0=ALU.mult))
            p.vop(lambda v: v.tensor_copy(out=pwk[:], in_=prb[:]))
            for _ in range(3):
                p.vop(lambda v: v.reduce_max(out=red[:], in_=pwk[:], axis=AX.X))
                p.vop(lambda v: v.tensor_scalar(out=msk[:], in0=pwk[:], scalar1=red[:], scalar2=None, op0=ALU.is_ge))
                p.vop(lambda v: v.select(out=pwk[:], mask=msk[:], on_true=ninf[:], on_false=pwk[:]))
            p.vop(lambda v: v.reduce_max(out=red[:], in_=pwk[:], axis=AX.X))
            p.vop(lambda v: v.tensor_scalar(out=msk[:], in0=prb[:], scalar1=red[:], scalar2=None, op0=ALU.is_ge))
            p.vop(lambda v: v.select(out=pwk[:], mask=msk[:], on_true=prb[:], on_false=zer[:]))
            p.vop(lambda v: v.reduce_sum(out=red[:], in_=pwk[:], axis=AX.X))
            p.vop(lambda v: v.reciprocal(out=red[:], in_=red[:]))
            p.vop(lambda v, tt=tt: v.tensor_scalar(out=cw_all[:, tt * E:(tt + 1) * E], in0=pwk[:],
                                                   scalar1=red[:], scalar2=0.8, op0=ALU.mult, op1=ALU.mult))
            p.wait("vector", "ts")
            p.vop(lambda v: v.tensor_tensor(out=xbb[:], in0=xbb[:], in1=dpb[:], op=ALU.mult))
            p.vop(lambda v: v.tensor_add(out=xbb[:], in0=xbb[:], in1=ybb[:]))
            transpose16(xbb, fmb, lambda kt: kt * 128)
            for nch in range(NCH):
                bank = pG[nch % 3]
                p.wait("tensor", "vs")
                p.op("tensor", "ts", gemm16(
                    bank,
                    lambda kt: fmb[:, kt * 128:(kt + 1) * 128],
                    lambda kt, nch=nch: wbig[:, kt * D + nch * 512: kt * D + (nch + 1) * 512]))
                p.wait("vector", "ts")
                p.wait("vector", "st")
                p.vop(lambda v, bank=bank: v.tensor_copy(out=sA[:], in_=bank[:]))
                p.wait("sync", "vs")
                p.dma("sync", "st", lambda sy, tt=tt, nch=nch: sy.dma_start(
                    out=at1_hbm.ap()[tt][:, nch * 512:(nch + 1) * 512], in_=sA[:]))

        moe = U1
        for th in range(2):
            tts = [th * 8 + i for i in range(8)]
            for i, tt in enumerate(tts):
                p.wait("gpsimd", "vs")
                p.wait("gpsimd", "ts")
                p.dma("gpsimd", "lx", lambda g, tt=tt: g.dma_start(out=xbb[:], in_=xn1_r[tt]))
                p.wait("vector", "lx")
                p.wait("tensor", "lx")
                transpose16(xbb, xnh, lambda kt, i=i: kt * 1024 + i * 128)
            for e in range(E):
                p.wait("gpsimd", "ts")
                for kt in range(KT):
                    p.dma2("gpsimd", "lw", lambda g, e=e, kt=kt: g.dma_start(out=wbig[:, kt * HE:(kt + 1) * HE], in_=we1b[e, kt]))
                    p.dma2("gpsimd", "lw", lambda g, e=e, kt=kt: g.dma_start(out=wbig[:, KT * HE + kt * HE: KT * HE + (kt + 1) * HE], in_=we3b[e, kt]))
                p.wait("tensor", "lw")
                for tokch in range(2):
                    up_evac_vs = [p.cnt["vs"]] * 2
                    for ht in range(8):
                        hp = ht % 2
                        bg, bl = pG[2 * hp], pG[2 * hp + 1]
                        p.wait("tensor", "vs", up_evac_vs[hp])
                        p.op("tensor", "ts", gemm16(
                            bg,
                            lambda kt, ht=ht: wbig[:, kt * HE + ht * 128: kt * HE + (ht + 1) * 128],
                            lambda kt, tokch=tokch: xnh[:, kt * 1024 + tokch * 512: kt * 1024 + (tokch + 1) * 512]))
                        p.op("tensor", "ts", gemm16(
                            bl,
                            lambda kt, ht=ht: wbig[:, KT * HE + kt * HE + ht * 128: KT * HE + kt * HE + (ht + 1) * 128],
                            lambda kt, tokch=tokch: xnh[:, kt * 1024 + tokch * 512: kt * 1024 + (tokch + 1) * 512]))
                        p.wait("scalar", "ts")
                        p.wait("scalar", "vs")
                        p.sop(lambda s, bg=bg: s.activation(out=sgb[:], in_=bg[:], func=AF.Silu))
                        p.wait("vector", "ts")
                        p.wait("vector", "ss")
                        p.vop(lambda v, ht=ht, bl=bl: v.tensor_copy(
                            out=gbuf[:, ht * 512:(ht + 1) * 512], in_=bl[:]))
                        p.vop(lambda v, ht=ht: v.tensor_mul(
                            out=gbuf[:, ht * 512:(ht + 1) * 512],
                            in0=sgb[:], in1=gbuf[:, ht * 512:(ht + 1) * 512]))
                        up_evac_vs[hp] = p.cnt["vs"]
                    for dch in range(NCH):
                        p.wait("gpsimd", "ts")
                        for ht in range(8):
                            p.dma2("gpsimd", "lw", lambda g, e=e, dch=dch, ht=ht: g.dma_start(
                                out=we2s[:, ht * 512:(ht + 1) * 512], in_=we2b[e, dch, ht]))
                        p.wait("tensor", "lw")
                        dn_evac_vs = [p.cnt["vs"]] * 2
                        for tb in range(4):
                            q = tokch * 4 + tb
                            col = (th * 8 + q) * E + e
                            bp = pG[4 + tb % 2]
                            p.wait("tensor", "vs", dn_evac_vs[tb % 2])
                            p.op("tensor", "ts", gemm16(
                                bp,
                                lambda ht, tb=tb: gbuf[:, ht * 512 + tb * 128: ht * 512 + (tb + 1) * 128],
                                lambda ht: we2s[:, ht * 512:(ht + 1) * 512], nk=8))
                            p.wait("vector", "ts")
                            if e == 0:
                                p.vop(lambda v, q=q, dch=dch, col=col, bp=bp: v.tensor_scalar(
                                    out=moe[:, q * D + dch * 512: q * D + (dch + 1) * 512],
                                    in0=bp[:], scalar1=cw_all[:, col:col + 1], scalar2=None, op0=ALU.mult))
                            else:
                                p.vop(lambda v, q=q, dch=dch, col=col, bp=bp: v.scalar_tensor_tensor(
                                    out=moe[:, q * D + dch * 512: q * D + (dch + 1) * 512],
                                    in0=bp[:], scalar=cw_all[:, col:col + 1],
                                    in1=moe[:, q * D + dch * 512: q * D + (dch + 1) * 512],
                                    op0=ALU.mult, op1=ALU.add))
                            dn_evac_vs[tb % 2] = p.cnt["vs"]
            p.wait("sync", "vs")
            for i, tt in enumerate(tts):
                p.dma("sync", "st", lambda sy, i=i, tt=tt: sy.dma_start(out=moe_hbm.ap()[tt], in_=moe[:, i * D:(i + 1) * D]))
            shfm = U1.ap().bitcast(BF16)
            p.wait("vector", "st")
            for hsch in range(8):
                p.wait("gpsimd", "ts")
                for kt in range(KT):
                    p.dma2("gpsimd", "lw", lambda g, hsch=hsch, kt=kt: g.dma_start(out=wbig[:, kt * 512:(kt + 1) * 512], in_=ws1b[hsch, kt]))
                    p.dma2("gpsimd", "lw", lambda g, hsch=hsch, kt=kt: g.dma_start(out=wbig[:, 8192 + kt * 512: 8192 + (kt + 1) * 512], in_=ws3b[hsch, kt]))
                p.wait("tensor", "lw")
                for i, tt in enumerate(tts):
                    p.wait("tensor", "vs")
                    p.wait("tensor", "ss")
                    p.op("tensor", "ts", gemm16(
                        pG[0],
                        lambda kt, i=i: xnh[:, kt * 1024 + i * 128: kt * 1024 + (i + 1) * 128],
                        lambda kt: wbig[:, kt * 512:(kt + 1) * 512]))
                    p.op("tensor", "ts", gemm16(
                        pG[1],
                        lambda kt, i=i: xnh[:, kt * 1024 + i * 128: kt * 1024 + (i + 1) * 128],
                        lambda kt: wbig[:, 8192 + kt * 512: 8192 + (kt + 1) * 512]))
                    p.wait("scalar", "ts")
                    p.wait("scalar", "vs")
                    p.sop(lambda s: s.activation(out=sgb[:], in_=pG[0][:], func=AF.Silu))
                    p.wait("vector", "ts")
                    p.wait("vector", "ss")
                    p.vop(lambda v: v.tensor_copy(out=shb[:], in_=pG[1][:]))
                    p.vop(lambda v: v.tensor_mul(out=shb[:], in0=sgb[:], in1=shb[:]))
                    transpose16(shb, shfm, lambda j, i=i, hsch=hsch: i * 4096 + (hsch * 4 + j) * 128, ntile=4)
            for dch in range(NCH):
                p.wait("gpsimd", "ts")
                for hst in range(32):
                    p.dma2("gpsimd", "lw", lambda g, dch=dch, hst=hst: g.dma_start(
                        out=wbig[:, hst * 512:(hst + 1) * 512], in_=ws2b[dch, hst]))
                p.wait("tensor", "lw")
                for i, tt in enumerate(tts):
                    p.wait("gpsimd", "vs")
                    p.dma("gpsimd", "lx", lambda g, tt=tt, dch=dch: g.dma_start(out=mo[:], in_=moe_hbm.ap()[tt][:, dch * 512:(dch + 1) * 512]))
                    p.dma("gpsimd", "lx", lambda g, tt=tt, dch=dch: g.dma_start(out=sX[:], in_=x1_r[tt][:, dch * 512:(dch + 1) * 512]))
                    p.dma("gpsimd", "lx", lambda g, tt=tt, dch=dch: g.dma_start(out=sY[:], in_=at1_hbm.ap()[tt][:, dch * 512:(dch + 1) * 512]))
                    p.wait("tensor", "vs")
                    p.op("tensor", "ts", gemm16(
                        pG[2],
                        lambda hst, i=i: shfm[:, i * 4096 + hst * 128: i * 4096 + (hst + 1) * 128],
                        lambda hst: wbig[:, hst * 512:(hst + 1) * 512], nk=32))
                    p.wait("vector", "ts")
                    p.wait("vector", "lx")
                    p.wait("vector", "st")
                    p.vop(lambda v: v.scalar_tensor_tensor(out=mo[:], in0=pG[2][:], scalar=0.2, in1=mo[:], op0=ALU.mult, op1=ALU.add))
                    p.vop(lambda v: v.tensor_add(out=mo[:], in0=mo[:], in1=sX[:]))
                    p.vop(lambda v: v.tensor_add(out=mo[:], in0=mo[:], in1=sY[:]))
                    p.wait("sync", "vs")
                    p.dma("sync", "st", lambda sy, tt=tt, dch=dch: sy.dma_start(
                        out=out_r[tt][:, dch * 512:(dch + 1) * 512], in_=mo[:]))
        p.wait("sync", "st")
        block = es.enter_context(nc.Block())
        p.attach(block)
    return nc


def run_F(inputs, y_tm, x1, xn1):
    if "F" not in _CACHE:
        _CACHE["F"] = build_F()
    opwTb = np.ascontiguousarray(inputs["L1_attn_out_proj_w"].T.reshape(KT, 128, D))
    wrb = np.ascontiguousarray(inputs["L1_ffn_w_router"].reshape(KT, 128, E))
    we1b = np.ascontiguousarray(inputs["L1_ffn_we1"].reshape(E, KT, 128, HE))
    we3b = np.ascontiguousarray(inputs["L1_ffn_we3"].reshape(E, KT, 128, HE))
    we2b = np.ascontiguousarray(
        inputs["L1_ffn_we2"].reshape(E, 8, 128, NCH, 512).transpose(0, 3, 1, 2, 4))
    ws1b = np.ascontiguousarray(
        inputs["L1_ffn_ws1"].reshape(KT, 128, 8, 512).transpose(2, 0, 1, 3))
    ws3b = np.ascontiguousarray(
        inputs["L1_ffn_ws3"].reshape(KT, 128, 8, 512).transpose(2, 0, 1, 3))
    ws2b = np.ascontiguousarray(
        inputs["L1_ffn_ws2"].reshape(32, 128, NCH, 512).transpose(2, 0, 1, 3))
    maps = []
    for c in range(8):
        sl = slice(c * MTOK, (c + 1) * MTOK)
        maps.append(dict(
            ys=np.ascontiguousarray(y_tm[sl]), xn1s=np.ascontiguousarray(xn1[sl]),
            x1s=np.ascontiguousarray(x1[sl]),
            dpr=inputs["L1_attn_D_param"].reshape(1, D),
            opwTb=opwTb, wrb=wrb, we1b=we1b, we3b=we3b, we2b=we2b,
            ws1b=ws1b, ws3b=ws3b, ws2b=ws2b,
            iden=np.eye(128, dtype=np.float32)))
    res = run_bass_kernel_spmd(_CACHE["F"], maps, core_ids=list(range(8)))
    return res.results


def kernel(**inputs):
    I = inputs
    x0 = np.asarray(I["L0_attn_hidden_states"], np.float32).reshape(B * T, D)

    resA = run_A(I)
    vpr = np.concatenate([np.asarray(resA[c]["vpr_o"]) for c in range(8)], 0).reshape(B, T, D)
    cumP = np.concatenate([np.asarray(resA[c]["cumP_o"]) for c in range(8)], 0).reshape(B, T, D)
    v0 = np.asarray(resA[0]["v0_o"])[0]
    xn0 = np.concatenate([np.asarray(resA[c]["xn0_o"]) for c in range(8)], 0)
    ffn0 = np.concatenate(
        [np.concatenate([np.asarray(resA[c]["ffn0_o"])[i] for i in range(NCH)], 1) for c in range(8)], 0)
    vfull = np.empty((B, TP1, D), np.float32); vfull[:, 0] = v0; vfull[:, 1:] = vpr
    cfull = np.empty((B, TP1, D), np.float32); cfull[:, 0] = 0.0; cfull[:, 1:] = cumP

    resB = run_B(vfull, cfull)
    loghs = [np.asarray(resB[c]["logh_o"]) for c in range(8)]
    parts = np.stack([np.asarray(resB[c]["part_o"])[0] for c in range(8)])

    resC = run_C(loghs, parts)
    h_tm = np.empty((B, T, D), np.float32)
    for c in range(8):
        b, j = c // 2, c % 2
        h_tm[b, :, j * 1024:(j + 1) * 1024] = np.asarray(resC[c]["h_o"]).reshape(1024, T).T
    h_tm = h_tm.reshape(B * T, D)

    resD = run_D(I, h_tm, xn0, x0, ffn0)
    x1 = np.concatenate([np.asarray(resD[c]["x1_o"]) for c in range(8)], 0)
    xn1 = np.concatenate([np.asarray(resD[c]["xn1_o"]) for c in range(8)], 0)
    delta = np.concatenate([np.asarray(resD[c]["delta_o"]) for c in range(8)], 0)
    dx = np.concatenate([np.asarray(resD[c]["dx_o"]) for c in range(8)], 0)
    Bs = np.concatenate([np.asarray(resD[c]["Bssm_o"]) for c in range(8)], 0)
    Cs = np.concatenate([np.asarray(resD[c]["Cssm_o"]) for c in range(8)], 0)

    resE = run_E(I, delta, dx, Bs, Cs)
    y_tm = np.empty((B, T, D), np.float32)
    for c in range(8):
        b, j = c // 2, c % 2
        y_tm[b, :, j * 1024:(j + 1) * 1024] = np.asarray(resE[c]["y_o"]).reshape(1024, T).T
    y_tm = y_tm.reshape(B * T, D)

    resF = run_F(I, y_tm, x1, xn1)
    out = np.concatenate([np.asarray(resF[c]["out_o"]) for c in range(8)], 0)
    return out.reshape(B, T, D).astype(np.float32)


# version 07 (final)
# speedup vs baseline: 1.1620x; 1.1620x over previous
import numpy as np
from contextlib import ExitStack
import ml_dtypes

import concourse.bass as bass
import concourse.mybir as mybir
from concourse.bass_utils import run_bass_kernel_spmd

F32 = mybir.dt.float32
BF16 = mybir.dt.bfloat16
U8 = mybir.dt.uint8
AF = mybir.ActivationFunctionType
ALU = mybir.AluOpType
AX = mybir.AxisListType
NBF = ml_dtypes.bfloat16

B, T, D = 4, 4096, 2048
HID = 8192
NS, RK = 128, 128
E, HE, HS = 16, 1024, 4096
MTOK = 2048
TT = MTOK // 128
KT = D // 128
NCH = D // 512
HCHN = HID // 128
LN_EPS = 1e-5


def hadamard(n):
    h = np.array([[1.0]], np.float64)
    while h.shape[0] < n:
        h = np.block([[h, h], [h, -h]])
    return h


class Prog:
    ENGS = ("tensor", "vector", "scalar", "gpsimd", "sync")

    def __init__(self, nc, es):
        self.nc = nc
        self.es = es
        self.q = {e: [] for e in self.ENGS}
        self.sems = {}
        self.cnt = {}

    def sem(self, name):
        self.sems[name] = self.es.enter_context(self.nc.semaphore(name))
        self.cnt[name] = 0
        return self.sems[name]

    def do(self, eng, fn):
        self.q[eng].append(fn)

    def wait(self, eng, sname, val=None):
        v = self.cnt[sname] if val is None else val
        if v <= 0:
            return
        s = self.sems[sname]
        self.q[eng].append(lambda e, s=s, v=v: e.wait_ge(s, v))

    def inc(self, sname, k=1):
        self.cnt[sname] += k
        return self.sems[sname]

    def dma(self, eng, sname, fn):
        if self.cnt[sname] > 0:
            self.wait(eng, sname)
        sem = self.inc(sname, 16)
        self.do(eng, lambda e, fn=fn, sem=sem: fn(e).then_inc(sem, 16))

    OWN = {"vector": "vs", "scalar": "ss", "tensor": "ts"}

    def dma2(self, eng, sname, fn):
        sem = self.inc(sname, 16)
        self.do(eng, lambda e, fn=fn, sem=sem: fn(e).then_inc(sem, 16))

    def op(self, eng, sname, fn):
        own = self.OWN.get(eng)
        if own is not None and eng != "tensor" and self.cnt[own] > 0:
            self.wait(eng, own)
        sem = self.inc(sname, 1)
        self.do(eng, lambda e, fn=fn, sem=sem: fn(e).then_inc(sem, 1))

    def vop(self, fn):
        self.op("vector", "vs", fn)

    def sop(self, fn):
        self.op("scalar", "ss", fn)

    def attach(self, block):
        for eng in self.ENGS:
            q = self.q[eng]
            getattr(block, eng)(lambda e, q=q: [f(e) for f in q])


def reg_act_consts(nc, es, p, cstap, values):
    cb = es.enter_context(nc.sbuf_tensor("cstt", [128, len(values)], F32))
    for i, v in enumerate(values):
        p.dma("gpsimd", "ld", lambda g, i=i: g.dma_start(
            out=cb[:, i:i + 1], in_=cstap[:, i:i + 1].partition_broadcast(128)))
        nc.const_aps.register(F32, float(v), cb[:, i:i + 1])


def gemm16(bank, lhs_col, rhs_col, nk=KT):
    def run(te, bank=bank, lhs_col=lhs_col, rhs_col=rhs_col, nk=nk):
        for kt in range(nk):
            ins = te.matmul(bank[:], lhs_col(kt), rhs_col(kt),
                            start=(kt == 0), stop=(kt == nk - 1))
        return ins
    return run


def sbuf(nc, es, name, shape, dtype=F32):
    return es.enter_context(nc.sbuf_tensor(name, shape, dtype))


def psum(nc, es, name, shape, dtype=F32):
    return es.enter_context(nc.psum_tensor(name, shape, dtype))



def build_A():
    nc = bass.Bass("TRN2", target_bir_lowering=False)
    x0s = nc.dram_tensor("x0s", [MTOK, D], F32, kind="ExternalInput").ap()
    wzTb = nc.dram_tensor("wzTb", [KT, 128, D], F32, kind="ExternalInput").ap()
    whTb = nc.dram_tensor("whTb", [KT, 128, D], F32, kind="ExternalInput").ap()
    bzr = nc.dram_tensor("bzr", [1, D], F32, kind="ExternalInput").ap()
    bhr = nc.dram_tensor("bhr", [1, D], F32, kind="ExternalInput").ap()
    h0r = nc.dram_tensor("h0r", [1, D], F32, kind="ExternalInput").ap()
    w1Tb = nc.dram_tensor("w1Tb", [HCHN, KT, 128, 128], F32, kind="ExternalInput").ap()
    w3Tb = nc.dram_tensor("w3Tb", [HCHN, KT, 128, 128], F32, kind="ExternalInput").ap()
    w2T = nc.dram_tensor("w2T", [HID, D], F32, kind="ExternalInput").ap()
    h64s = nc.dram_tensor("h64s", [64, 64], F32, kind="ExternalInput").ap()
    h128 = nc.dram_tensor("h128", [128, 128], F32, kind="ExternalInput").ap()
    iden = nc.dram_tensor("iden", [128, 128], F32, kind="ExternalInput").ap()
    cst = nc.dram_tensor("cst", [1, 2], F32, kind="ExternalInput").ap()

    xn0_o = nc.dram_tensor("xn0_o", [MTOK, D], F32, kind="ExternalOutput").ap()
    vpr_o = nc.dram_tensor("vpr_o", [MTOK, D], F32, kind="ExternalOutput").ap()
    cumP_o = nc.dram_tensor("cumP_o", [MTOK, D], F32, kind="ExternalOutput").ap()
    v0_o = nc.dram_tensor("v0_o", [1, D], F32, kind="ExternalOutput").ap()
    ffn0_o = nc.dram_tensor("ffn0_o", [NCH, MTOK, 512], F32, kind="ExternalOutput").ap()

    k_hbm = nc.dram_tensor("k_hbm", [TT, 128, D], F32)
    hidA_hbm = nc.dram_tensor("hidA_hbm", [TT, HCHN, 128, 128], BF16)
    w2pp_hbm = nc.dram_tensor("w2pp_hbm", [NCH, HID, 512], BF16)

    x0_r = x0s.rearrange("(tt p) d -> tt p d", p=128)
    xn0_r = xn0_o.rearrange("(tt p) d -> tt p d", p=128)
    vpr_r = vpr_o.rearrange("(tt p) d -> tt p d", p=128)
    cumP_r = cumP_o.rearrange("(tt p) d -> tt p d", p=128)

    with ExitStack() as es:
        p = Prog(nc, es)
        for s in ("ld", "st", "ts", "vs", "ss", "lwa", "lwb", "lp", "lha", "lhb"):
            p.sem(s)

        xn0_fm = sbuf(nc, es, "xn0_fm", [128, KT * MTOK], BF16)
        wmat = sbuf(nc, es, "wmat", [128, KT * D], BF16)
        bzb = sbuf(nc, es, "bzb", [128, D], BF16)
        bhb = sbuf(nc, es, "bhb", [128, D], BF16)
        h128b = sbuf(nc, es, "h128b", [128, 128], BF16)
        h64f = sbuf(nc, es, "h64f", [64, 64], F32)
        idb = sbuf(nc, es, "idb", [128, 128], BF16)
        xb0 = sbuf(nc, es, "xb0", [128, D], F32)
        s1 = sbuf(nc, es, "s1", [128, D], F32)
        s2 = sbuf(nc, es, "s2", [128, D], F32)
        s3 = sbuf(nc, es, "s3", [128, D], F32)
        s4 = sbuf(nc, es, "s4", [128, D], F32)
        xnb = sbuf(nc, es, "xnb", [128, D], BF16)
        ksb = sbuf(nc, es, "ksb", [128, D], F32)
        ptb = sbuf(nc, es, "ptb", [128, D], F32)
        mask_t = sbuf(nc, es, "mask", [128, D], U8)
        red = sbuf(nc, es, "red", [128, 1], F32)
        red2 = sbuf(nc, es, "red2", [128, 1], F32)
        sgb = sbuf(nc, es, "sgb", [128, 512], BF16)
        ulb = sbuf(nc, es, "ulb", [128, 512], BF16)
        hidb = sbuf(nc, es, "hidb", [128, 512], BF16)
        hidAb = sbuf(nc, es, "hidAb", [128, 512], BF16)
        fout = sbuf(nc, es, "fout", [128, 512], F32)

        pG = [psum(nc, es, f"pG{i}", [128, 512], F32) for i in range(3)]
        pTr = psum(nc, es, "pTr", [128, 128], BF16)
        pTr2 = psum(nc, es, "pTr2", [128, 128], BF16)
        pF = psum(nc, es, "pF", [64, 512], F32)

        reg_act_consts(nc, es, p, cst, [-1.0, 0.5])
        p.dma("gpsimd", "ld", lambda g: g.dma_start(out=bzb[:], in_=bzr.partition_broadcast(128)))
        p.dma("gpsimd", "ld", lambda g: g.dma_start(out=bhb[:], in_=bhr.partition_broadcast(128)))
        p.dma("gpsimd", "ld", lambda g: g.dma_start(out=h128b[:], in_=h128))
        p.dma("gpsimd", "ld", lambda g: g.dma_start(out=h64f[:], in_=h64s))
        p.dma("gpsimd", "ld", lambda g: g.dma_start(out=idb[:], in_=iden))

        for tt in range(TT):
            p.wait("gpsimd", "vs")
            p.dma("gpsimd", "ld", lambda g, tt=tt: g.dma_start(out=xb0[:], in_=x0_r[tt]))
            p.wait("vector", "ld")
            p.wait("vector", "st")
            p.wait("vector", "ts")
            p.vop(lambda v: v.reduce_sum(out=red[:], in_=xb0[:], axis=AX.X))
            p.vop(lambda v: v.tensor_scalar(out=red[:], in0=red[:], scalar1=1.0 / D, scalar2=None, op0=ALU.mult))
            p.vop(lambda v: v.tensor_scalar(out=s1[:], in0=xb0[:], scalar1=red[:], scalar2=None, op0=ALU.subtract))
            p.wait("scalar", "vs")
            p.sop(lambda s: s.activation(out=s2[:], in_=s1[:], func=AF.Square, accum_out=red2[:]))
            p.wait("vector", "ss")
            p.vop(lambda v: v.tensor_scalar(out=red2[:], in0=red2[:], scalar1=1.0 / D, scalar2=LN_EPS, op0=ALU.mult, op1=ALU.add))
            p.wait("scalar", "vs")
            p.sop(lambda s: s.activation(out=red2[:], in_=red2[:], func=AF.Sqrt))
            p.wait("vector", "ss")
            p.vop(lambda v: v.reciprocal(out=red2[:], in_=red2[:]))
            p.vop(lambda v: v.tensor_scalar(out=s3[:], in0=s1[:], scalar1=red2[:], scalar2=None, op0=ALU.mult))
            p.vop(lambda v: v.tensor_copy(out=xnb[:], in_=s3[:]))
            p.wait("sync", "vs")
            p.dma("sync", "st", lambda sy, tt=tt: sy.dma_start(out=xn0_r[tt], in_=s3[:]))
            p.wait("tensor", "vs")
            tr_evac = [p.cnt["vs"]] * 2
            for kt in range(KT):
                b = pTr if kt % 2 == 0 else pTr2
                p.wait("tensor", "vs", tr_evac[kt % 2])
                p.op("tensor", "ts", lambda te, kt=kt, b=b: te.transpose(b[:], xnb[:, kt * 128:(kt + 1) * 128], idb[:]))
                p.wait("vector", "ts")
                p.vop(lambda v, kt=kt, tt=tt, b=b: v.tensor_copy(
                    out=xn0_fm[:, kt * MTOK + tt * 128: kt * MTOK + (tt + 1) * 128], in_=b[:]))
                tr_evac[kt % 2] = p.cnt["vs"]

        for kt in range(KT):
            p.dma("gpsimd", "ld", lambda g, kt=kt: g.dma_start(out=wmat[:, kt * D:(kt + 1) * D], in_=wzTb[kt]))
        p.wait("tensor", "ld")

        for tt in range(TT):
            p.wait("vector", "st")
            for nch in range(NCH):
                bank = pG[nch % 3]
                p.wait("tensor", "vs")
                p.op("tensor", "ts", gemm16(
                    bank,
                    lambda kt, tt=tt: xn0_fm[:, kt * MTOK + tt * 128: kt * MTOK + (tt + 1) * 128],
                    lambda kt, nch=nch: wmat[:, kt * D + nch * 512: kt * D + (nch + 1) * 512]))
                p.wait("vector", "ts")
                p.vop(lambda v, bank=bank, nch=nch: v.tensor_tensor(
                    out=ksb[:, nch * 512:(nch + 1) * 512], in0=bank[:],
                    in1=bzb[:, nch * 512:(nch + 1) * 512], op=ALU.add))
            p.wait("sync", "vs")
            p.dma("sync", "st", lambda sy, tt=tt: sy.dma_start(out=k_hbm.ap()[tt], in_=ksb[:]))

        p.wait("gpsimd", "ts")
        for kt in range(KT):
            p.dma("gpsimd", "ld", lambda g, kt=kt: g.dma_start(out=wmat[:, kt * D:(kt + 1) * D], in_=whTb[kt]))

        p.dma("gpsimd", "ld", lambda g: g.dma_start(out=xb0[0:1, :], in_=h0r))
        p.wait("scalar", "ld")
        p.wait("scalar", "vs")
        p.sop(lambda s: s.activation(out=s1[0:1, :], in_=xb0[0:1, :], func=AF.Abs))
        p.sop(lambda s: s.activation(out=s1[0:1, :], in_=s1[0:1, :], func=AF.Exp, scale=-1.0))
        p.sop(lambda s: s.activation(out=s1[0:1, :], in_=s1[0:1, :], func=AF.Ln, bias=1.0))
        p.sop(lambda s: s.activation(out=s2[0:1, :], in_=xb0[0:1, :], func=AF.Relu))
        p.sop(lambda s: s.activation(out=s2[0:1, :], in_=s2[0:1, :], func=AF.Ln, bias=0.5))
        p.wait("vector", "ss")
        p.vop(lambda v: v.scalar_tensor_tensor(out=s1[0:1, :], in0=xb0[0:1, :], scalar=0.0, in1=s1[0:1, :], op0=ALU.min, op1=ALU.subtract))
        p.vop(lambda v: v.tensor_single_scalar(out=mask_t[0:1, :], in_=xb0[0:1, :], scalar=0.0, op=ALU.is_ge))
        p.vop(lambda v: v.select(out=s4[0:1, :], mask=mask_t[0:1, :], on_true=s2[0:1, :], on_false=s1[0:1, :]))
        p.wait("sync", "vs")
        p.dma("sync", "st", lambda sy: sy.dma_start(out=v0_o, in_=s4[0:1, :]))

        p.wait("tensor", "ld")
        for tt in range(TT):
            p.wait("vector", "ss")
            p.wait("vector", "st")
            for nch in range(NCH):
                bank = pG[nch % 3]
                p.wait("tensor", "vs")
                p.op("tensor", "ts", gemm16(
                    bank,
                    lambda kt, tt=tt: xn0_fm[:, kt * MTOK + tt * 128: kt * MTOK + (tt + 1) * 128],
                    lambda kt, nch=nch: wmat[:, kt * D + nch * 512: kt * D + (nch + 1) * 512]))
                p.wait("vector", "ts")
                p.vop(lambda v, bank=bank, nch=nch: v.tensor_tensor(
                    out=ptb[:, nch * 512:(nch + 1) * 512], in0=bank[:],
                    in1=bhb[:, nch * 512:(nch + 1) * 512], op=ALU.add))
            p.wait("gpsimd", "st")
            p.wait("gpsimd", "ss")
            p.dma("gpsimd", "ld", lambda g, tt=tt: g.dma_start(out=ksb[:], in_=k_hbm.ap()[tt]))
            p.wait("scalar", "ld")
            p.wait("scalar", "vs")
            p.sop(lambda s: s.activation(out=s1[:], in_=ksb[:], func=AF.Abs))
            p.sop(lambda s: s.activation(out=s1[:], in_=s1[:], func=AF.Exp, scale=-1.0))
            p.sop(lambda s: s.activation(out=s1[:], in_=s1[:], func=AF.Ln, bias=1.0))
            p.sop(lambda s: s.activation(out=s2[:], in_=ptb[:], func=AF.Abs))
            p.sop(lambda s: s.activation(out=s2[:], in_=s2[:], func=AF.Exp, scale=-1.0))
            p.sop(lambda s: s.activation(out=s2[:], in_=s2[:], func=AF.Ln, bias=1.0))
            p.sop(lambda s: s.activation(out=s3[:], in_=ptb[:], func=AF.Relu))
            p.sop(lambda s: s.activation(out=s3[:], in_=s3[:], func=AF.Ln, bias=0.5))
            p.wait("vector", "ss")
            p.vop(lambda v: v.scalar_tensor_tensor(out=s4[:], in0=ksb[:], scalar=0.0, in1=s1[:], op0=ALU.min, op1=ALU.subtract))
            p.vop(lambda v: v.scalar_tensor_tensor(out=xb0[:], in0=ksb[:], scalar=0.0, in1=s1[:], op0=ALU.max, op1=ALU.add))
            p.vop(lambda v: v.scalar_tensor_tensor(out=s2[:], in0=ptb[:], scalar=0.0, in1=s2[:], op0=ALU.min, op1=ALU.subtract))
            p.vop(lambda v: v.tensor_single_scalar(out=mask_t[:], in_=ptb[:], scalar=0.0, op=ALU.is_ge))
            p.vop(lambda v: v.select(out=s2[:], mask=mask_t[:], on_true=s3[:], on_false=s2[:]))
            p.vop(lambda v: v.tensor_add(out=s4[:], in0=s4[:], in1=s2[:]))
            p.vop(lambda v: v.tensor_tensor_scan(out=s2[:], data0=xb0[:], data1=xb0[:], initial=0.0, op0=ALU.add, op1=ALU.bypass))
            p.vop(lambda v: v.tensor_add(out=s3[:], in0=s4[:], in1=s2[:]))
            p.wait("sync", "vs")
            p.dma("sync", "st", lambda sy, tt=tt: sy.dma_start(out=cumP_r[tt], in_=s2[:]))
            p.dma("sync", "st", lambda sy, tt=tt: sy.dma_start(out=vpr_r[tt], in_=s3[:]))

        w2T_r = w2T.rearrange("(hi lo) d -> lo hi d", lo=128)
        w2pp_r = [w2pp_hbm.ap()[dch].rearrange("(hi l) n -> l hi n", l=128) for dch in range(NCH)]
        p.wait("gpsimd", "vs")
        p.wait("gpsimd", "ss")
        for lo in range(128):
            p.wait("gpsimd", "ts")
            p.dma("gpsimd", "ld", lambda g, lo=lo: g.dma_start(out=s1[0:64, :], in_=w2T_r[lo]))
            p.wait("tensor", "ld")
            p.wait("vector", "st")
            for dch in range(NCH):
                p.wait("tensor", "vs")
                p.op("tensor", "ts", lambda te, dch=dch: te.matmul(pF[:], h64f[:], s1[0:64, dch * 512:(dch + 1) * 512], start=True, stop=True))
                p.wait("vector", "ts")
                p.vop(lambda v, dch=dch: v.tensor_copy(out=xnb[0:64, dch * 512:(dch + 1) * 512], in_=pF[:]))
            p.wait("sync", "vs")
            for dch in range(NCH):
                p.dma("sync", "st", lambda sy, lo=lo, dch=dch: sy.dma_start(out=w2pp_r[dch][lo], in_=xnb[0:64, dch * 512:(dch + 1) * 512]))

        ts_after_gemm = {}
        for hch in range(HCHN):
            par = hch % 2
            base = par * 2 * KT * 128
            sem = "lwa" if par == 0 else "lwb"
            if hch >= 2:
                p.wait("gpsimd", "ts", ts_after_gemm[hch - 2])
            else:
                p.wait("gpsimd", "ts")
            for kt in range(KT):
                p.dma2("gpsimd", sem, lambda g, hch=hch, kt=kt, base=base: g.dma_start(out=wmat[:, base + kt * 128: base + (kt + 1) * 128], in_=w1Tb[hch, kt]))
                p.dma2("gpsimd", sem, lambda g, hch=hch, kt=kt, base=base: g.dma_start(out=wmat[:, base + KT * 128 + kt * 128: base + KT * 128 + (kt + 1) * 128], in_=w3Tb[hch, kt]))
            p.wait("tensor", sem)
            for tokch in range(NCH):
                p.wait("tensor", "vs")
                p.wait("tensor", "ss")
                p.op("tensor", "ts", gemm16(
                    pG[0],
                    lambda kt, base=base: wmat[:, base + kt * 128: base + (kt + 1) * 128],
                    lambda kt, tokch=tokch: xn0_fm[:, kt * MTOK + tokch * 512: kt * MTOK + (tokch + 1) * 512]))
                p.op("tensor", "ts", gemm16(
                    pG[1],
                    lambda kt, base=base: wmat[:, base + KT * 128 + kt * 128: base + KT * 128 + (kt + 1) * 128],
                    lambda kt, tokch=tokch: xn0_fm[:, kt * MTOK + tokch * 512: kt * MTOK + (tokch + 1) * 512]))
                p.wait("scalar", "ts")
                p.wait("scalar", "st")
                p.sop(lambda s: s.activation(out=sgb[:], in_=pG[0][:], func=AF.Silu))
                p.wait("vector", "ts")
                p.vop(lambda v: v.tensor_copy(out=ulb[:], in_=pG[1][:]))
                p.wait("vector", "ss")
                p.vop(lambda v: v.tensor_mul(out=hidb[:], in0=sgb[:], in1=ulb[:]))
                p.wait("tensor", "vs")
                p.op("tensor", "ts", lambda te: te.matmul(pG[2][:], h128b[:], hidb[:], start=True, stop=True))
                p.wait("vector", "ts")
                p.wait("vector", "st")
                p.vop(lambda v: v.tensor_copy(out=hidAb[:], in_=pG[2][:]))
                p.wait("sync", "vs")
                for i in range(4):
                    p.dma("sync", "st", lambda sy, hch=hch, tokch=tokch, i=i: sy.dma_start(
                        out=hidA_hbm.ap()[tokch * 4 + i, hch],
                        in_=hidAb[:, i * 128:(i + 1) * 128]))
            ts_after_gemm[hch] = p.cnt["ts"]

        w2ppb = wmat
        hidAl = xn0_fm
        ts_after_a4 = {}
        for dch in range(NCH):
            p.wait("gpsimd", "ts")
            p.wait("gpsimd", "st")
            for h in range(HCHN):
                p.dma2("gpsimd", "lp", lambda g, dch=dch, h=h: g.dma_start(
                    out=w2ppb[:, h * 512:(h + 1) * 512],
                    in_=w2pp_hbm.ap()[dch, h * 128:(h + 1) * 128, :]))
            lp_cnt = p.cnt["lp"]
            for tt in range(TT):
                par = tt % 2
                hbase = par * HCHN * 128
                hsem = "lha" if par == 0 else "lhb"
                key = (dch, tt)
                prev = (dch, tt - 2) if tt >= 2 else ((dch - 1, TT - 2 + tt) if dch >= 1 else None)
                if prev is not None and prev in ts_after_a4:
                    p.wait("gpsimd", "ts", ts_after_a4[prev])
                for h in range(HCHN):
                    p.dma2("gpsimd", hsem, lambda g, tt=tt, h=h, hbase=hbase: g.dma_start(
                        out=hidAl[:, hbase + h * 128: hbase + (h + 1) * 128],
                        in_=hidA_hbm.ap()[tt, h]))
                p.wait("tensor", hsem)
                p.wait("tensor", "lp", lp_cnt)
                p.wait("tensor", "vs")
                def a4run(te, hbase=hbase):
                    for h in range(HCHN):
                        ins = te.matmul(pG[0][:],
                                        hidAl[:, hbase + h * 128: hbase + (h + 1) * 128],
                                        w2ppb[:, h * 512:(h + 1) * 512],
                                        start=(h == 0), stop=(h == HCHN - 1))
                    return ins
                p.op("tensor", "ts", a4run)
                ts_after_a4[key] = p.cnt["ts"]
                p.wait("vector", "ts")
                p.wait("vector", "st")
                p.vop(lambda v: v.tensor_copy(out=fout[:], in_=pG[0][:]))
                p.wait("sync", "vs")
                p.dma("sync", "st", lambda sy, dch=dch, tt=tt: sy.dma_start(
                    out=ffn0_o[dch].rearrange("(tt p) n -> tt p n", p=128)[tt], in_=fout[:]))

        p.wait("sync", "st")
        p.wait("sync", "ld")
        block = es.enter_context(nc.Block())
        p.attach(block)
    return nc



_CACHE = {}


def _prep_A_host(inputs):
    Wz = inputs["L0_attn_W_z"]
    Wh = inputs["L0_attn_W_h"]
    W1 = inputs["L0_ffn_W1"]
    W3 = inputs["L0_ffn_W3"]
    W2 = inputs["L0_ffn_W2"]
    wzTb = np.ascontiguousarray(Wz.T.reshape(KT, 128, D))
    whTb = np.ascontiguousarray(Wh.T.reshape(KT, 128, D))
    w1Tb = np.ascontiguousarray(W1.T.reshape(KT, 128, HCHN, 128).transpose(2, 0, 1, 3))
    w3Tb = np.ascontiguousarray(W3.T.reshape(KT, 128, HCHN, 128).transpose(2, 0, 1, 3))
    w2T = np.ascontiguousarray(W2.T)
    h64s = (hadamard(64) / np.sqrt(HID)).astype(np.float32)
    h128c = hadamard(128).astype(np.float32)
    iden = np.eye(128, dtype=np.float32)
    bz = inputs["L0_attn_b_z"].reshape(1, D)
    bh = inputs["L0_attn_b_h"].reshape(1, D)
    h0 = inputs["L0_attn_h0"].reshape(1, D)
    x0 = inputs["L0_attn_hidden_states"].reshape(B * T, D)
    maps = []
    for c in range(8):
        maps.append(dict(
            x0s=np.ascontiguousarray(x0[c * MTOK:(c + 1) * MTOK]),
            wzTb=wzTb, whTb=whTb, bzr=bz, bhr=bh, h0r=h0,
            w1Tb=w1Tb, w3Tb=w3Tb, w2T=w2T, h64s=h64s, h128=h128c, iden=iden,
            cst=np.array([[-1.0, 0.5]], np.float32),
        ))
    return maps


def run_A(inputs):
    if "A" not in _CACHE:
        _CACHE["A"] = build_A()
    maps = _prep_A_host(inputs)
    res = run_bass_kernel_spmd(_CACHE["A"], maps, core_ids=list(range(8)))
    return res.results





TP1 = T + 1
HPC = 8


def build_B():
    nc = bass.Bass("TRN2", target_bir_lowering=False)
    vfm = nc.dram_tensor("vfm", [HPC, 128, TP1], F32, kind="ExternalInput").ap()
    cfm = nc.dram_tensor("cfm", [HPC, 128, TP1], F32, kind="ExternalInput").ap()
    onescol = nc.dram_tensor("onescol", [128, 1], F32, kind="ExternalInput").ap()
    logh_o = nc.dram_tensor("logh_o", [HPC, 128, TP1], F32, kind="ExternalOutput").ap()
    part_o = nc.dram_tensor("part_o", [1, 2], F32, kind="ExternalOutput").ap()

    with ExitStack() as es:
        p = Prog(nc, es)
        for s in ("ld", "st", "ts", "vs", "ss"):
            p.sem(s)
        vt = sbuf(nc, es, "vt", [128, TP1], F32)
        ct = sbuf(nc, es, "ct", [128, TP1], F32)
        Mt = sbuf(nc, es, "Mt", [128, TP1], F32)
        At = sbuf(nc, es, "At", [128, TP1], F32)
        Bt = sbuf(nc, es, "Bt", [128, TP1], F32)
        St = sbuf(nc, es, "St", [128, TP1], F32)
        r1 = sbuf(nc, es, "r1", [128, 1], F32)
        pacc = sbuf(nc, es, "pacc", [128, 2], F32)
        onesb = sbuf(nc, es, "onesb", [128, 1], F32)
        stat = sbuf(nc, es, "stat", [1, 2], F32)
        pS = psum(nc, es, "pS", [1, 2], F32)

        p.dma("gpsimd", "ld", lambda g: g.dma_start(out=onesb[:], in_=onescol))
        p.vop(lambda v: v.memset(pacc[:], 0.0))
        for ht in range(HPC):
            p.wait("gpsimd", "vs")
            p.wait("gpsimd", "ss")
            p.wait("gpsimd", "st")
            p.dma("gpsimd", "ld", lambda g, ht=ht: g.dma_start(out=vt[:], in_=vfm[ht]))
            p.dma("gpsimd", "ld", lambda g, ht=ht: g.dma_start(out=ct[:], in_=cfm[ht]))
            p.wait("vector", "ld")
            p.vop(lambda v: v.tensor_tensor_scan(out=Mt[:], data0=vt[:], data1=vt[:], initial=-3e38, op0=ALU.max, op1=ALU.max))
            p.vop(lambda v: v.memset(At[:, 0:1], 0.0))
            p.vop(lambda v: v.tensor_sub(out=At[:, 1:TP1], in0=Mt[:, 0:T], in1=Mt[:, 1:TP1]))
            p.vop(lambda v: v.tensor_sub(out=Bt[:], in0=vt[:], in1=Mt[:]))
            p.wait("scalar", "vs")
            p.sop(lambda s: s.activation(out=At[:], in_=At[:], func=AF.Exp))
            p.sop(lambda s: s.activation(out=Bt[:], in_=Bt[:], func=AF.Exp))
            p.wait("vector", "ss")
            p.vop(lambda v: v.tensor_tensor_scan(out=St[:], data0=At[:], data1=Bt[:], initial=0.0, op0=ALU.mult, op1=ALU.add))
            p.wait("scalar", "vs")
            p.sop(lambda s: s.activation(out=At[:], in_=St[:], func=AF.Ln))
            p.wait("vector", "ss")
            p.vop(lambda v: v.tensor_add(out=Bt[:], in0=At[:], in1=Mt[:]))
            p.vop(lambda v: v.tensor_sub(out=vt[:], in0=Bt[:], in1=ct[:]))
            p.wait("sync", "vs")
            p.dma("sync", "st", lambda sy, ht=ht: sy.dma_start(out=logh_o[ht], in_=vt[:]))
            p.vop(lambda v: v.reduce_sum(out=r1[:], in_=vt[:], axis=AX.X))
            p.vop(lambda v: v.tensor_add(out=pacc[:, 0:1], in0=pacc[:, 0:1], in1=r1[:]))
            p.wait("scalar", "vs")
            p.sop(lambda s: s.activation(out=St[:], in_=vt[:], func=AF.Square, accum_out=r1[:]))
            p.wait("vector", "ss")
            p.vop(lambda v: v.tensor_add(out=pacc[:, 1:2], in0=pacc[:, 1:2], in1=r1[:]))
        p.wait("tensor", "vs")
        p.wait("tensor", "ld")
        p.op("tensor", "ts", lambda te: te.matmul(pS[:], onesb[:], pacc[:], start=True, stop=True))
        p.wait("vector", "ts")
        p.vop(lambda v: v.tensor_copy(out=stat[:], in_=pS[:]))
        p.wait("sync", "vs")
        p.dma("sync", "st", lambda sy: sy.dma_start(out=part_o, in_=stat[:]))
        p.wait("sync", "st")
        block = es.enter_context(nc.Block())
        p.attach(block)
    return nc



def build_C():
    nc = bass.Bass("TRN2", target_bir_lowering=False)
    logh = nc.dram_tensor("logh", [HPC, 128, TP1], F32, kind="ExternalInput").ap()
    parts = nc.dram_tensor("parts", [8, 2], F32, kind="ExternalInput").ap()
    ones8 = nc.dram_tensor("ones8", [8, 1], F32, kind="ExternalInput").ap()
    onesr = nc.dram_tensor("onesr", [1, 128], F32, kind="ExternalInput").ap()
    h_o = nc.dram_tensor("h_o", [HPC, 128, T], F32, kind="ExternalOutput").ap()
    NTOT = float(B * TP1 * D)

    with ExitStack() as es:
        p = Prog(nc, es)
        for s in ("ld", "st", "ts", "vs", "ss"):
            p.sem(s)
        pb = sbuf(nc, es, "pb", [8, 2], F32)
        o8 = sbuf(nc, es, "o8", [8, 1], F32)
        orow = sbuf(nc, es, "orow", [1, 128], F32)
        stat = sbuf(nc, es, "stat", [1, 2], F32)
        sc1 = sbuf(nc, es, "sc1", [1, 2], F32)
        tmp = sbuf(nc, es, "tmp", [1, 1], F32)
        sc2 = sbuf(nc, es, "sc2", [128, 2], F32)
        lt = sbuf(nc, es, "lt", [128, TP1], F32)
        pS = psum(nc, es, "pS", [1, 2], F32)
        pB = psum(nc, es, "pB", [128, 2], F32)

        p.dma("gpsimd", "ld", lambda g: g.dma_start(out=pb[:], in_=parts))
        p.dma("gpsimd", "ld", lambda g: g.dma_start(out=o8[:], in_=ones8))
        p.dma("gpsimd", "ld", lambda g: g.dma_start(out=orow[:], in_=onesr))
        p.wait("tensor", "ld")
        p.op("tensor", "ts", lambda te: te.matmul(pS[:], o8[:], pb[:], start=True, stop=True))
        p.wait("vector", "ts")
        p.vop(lambda v: v.tensor_copy(out=stat[:], in_=pS[:]))
        p.vop(lambda v: v.tensor_scalar(out=sc1[:, 0:1], in0=stat[:, 0:1], scalar1=1.0 / NTOT, scalar2=None, op0=ALU.mult))
        p.vop(lambda v: v.tensor_mul(out=tmp[:], in0=stat[:, 0:1], in1=stat[:, 0:1]))
        p.vop(lambda v: v.tensor_scalar(out=tmp[:], in0=tmp[:], scalar1=1.0 / NTOT, scalar2=None, op0=ALU.mult))
        p.vop(lambda v: v.tensor_sub(out=tmp[:], in0=stat[:, 1:2], in1=tmp[:]))
        p.vop(lambda v: v.tensor_scalar(out=tmp[:], in0=tmp[:], scalar1=1.0 / (NTOT - 1.0), scalar2=None, op0=ALU.mult))
        p.wait("scalar", "vs")
        p.sop(lambda s: s.activation(out=tmp[:], in_=tmp[:], func=AF.Sqrt))
        p.wait("vector", "ss")
        p.vop(lambda v: v.reciprocal(out=sc1[:, 1:2], in_=tmp[:]))
        p.vop(lambda v: v.tensor_mul(out=sc1[:, 0:1], in0=sc1[:, 0:1], in1=sc1[:, 1:2]))
        p.vop(lambda v: v.tensor_scalar(out=sc1[:, 0:1], in0=sc1[:, 0:1], scalar1=-1.0, scalar2=None, op0=ALU.mult))
        p.wait("tensor", "vs")
        p.op("tensor", "ts", lambda te: te.matmul(pB[:], orow[:], sc1[:], start=True, stop=True))
        p.wait("vector", "ts")
        p.vop(lambda v: v.tensor_copy(out=sc2[:], in_=pB[:]))
        sem_sc = p.cnt["vs"]
        for ht in range(HPC):
            p.wait("gpsimd", "ss")
            p.dma("gpsimd", "ld", lambda g, ht=ht: g.dma_start(out=lt[:], in_=logh[ht]))
            p.wait("scalar", "ld")
            p.wait("scalar", "vs", sem_sc)
            p.wait("scalar", "st")
            p.sop(lambda s: s.activation(out=lt[:], in_=lt[:], func=AF.Exp, scale=sc2[:, 1:2], bias=sc2[:, 0:1]))
            p.wait("sync", "ss")
            p.dma("sync", "st", lambda sy, ht=ht: sy.dma_start(out=h_o[ht], in_=lt[:, 1:TP1]))
        p.wait("sync", "st")
        block = es.enter_context(nc.Block())
        p.attach(block)
    return nc


def run_B(vfull, cfull):
    if "B" not in _CACHE:
        _CACHE["B"] = build_B()
    maps = []
    ones = np.ones((128, 1), np.float32)
    for c in range(8):
        b, j = c // 2, c % 2
        hsl = slice(j * 1024, (j + 1) * 1024)
        vfm = np.ascontiguousarray(vfull[b, :, hsl].T).reshape(HPC, 128, TP1)
        cfm = np.ascontiguousarray(cfull[b, :, hsl].T).reshape(HPC, 128, TP1)
        maps.append(dict(vfm=vfm, cfm=cfm, onescol=ones))
    res = run_bass_kernel_spmd(_CACHE["B"], maps, core_ids=list(range(8)))
    return res.results


def run_C(loghs, parts):
    if "C" not in _CACHE:
        _CACHE["C"] = build_C()
    maps = []
    for c in range(8):
        maps.append(dict(logh=loghs[c], parts=parts,
                         ones8=np.ones((8, 1), np.float32),
                         onesr=np.ones((1, 128), np.float32)))
    res = run_bass_kernel_spmd(_CACHE["C"], maps, core_ids=list(range(8)))
    return res.results





def build_D():
    nc = bass.Bass("TRN2", target_bir_lowering=False)
    hs = nc.dram_tensor("hs", [MTOK, D], F32, kind="ExternalInput").ap()
    xn0s = nc.dram_tensor("xn0s", [MTOK, D], F32, kind="ExternalInput").ap()
    x0s = nc.dram_tensor("x0s", [MTOK, D], F32, kind="ExternalInput").ap()
    ffn0s = nc.dram_tensor("ffn0s", [MTOK, D], F32, kind="ExternalInput").ap()
    lnw = nc.dram_tensor("lnw", [1, D], F32, kind="ExternalInput").ap()
    lnb = nc.dram_tensor("lnb", [1, D], F32, kind="ExternalInput").ap()
    nwr = nc.dram_tensor("nwr", [1, D], F32, kind="ExternalInput").ap()
    nwcol = nc.dram_tensor("nwcol", [128, KT], F32, kind="ExternalInput").ap()
    spbr = nc.dram_tensor("spbr", [1, D], F32, kind="ExternalInput").ap()
    sgr = nc.dram_tensor("sgr", [1, D], F32, kind="ExternalInput").ap()
    dtbr = nc.dram_tensor("dtbr", [1, D], F32, kind="ExternalInput").ap()
    spwTb = nc.dram_tensor("spwTb", [KT, 128, D], F32, kind="ExternalInput").ap()
    xpwTb = nc.dram_tensor("xpwTb", [KT, 128, 384], F32, kind="ExternalInput").ap()
    dtwT = nc.dram_tensor("dtwT", [128, D], F32, kind="ExternalInput").ap()
    iden = nc.dram_tensor("iden", [128, 128], F32, kind="ExternalInput").ap()
    cst = nc.dram_tensor("cst", [1, 2], F32, kind="ExternalInput").ap()

    x1_o = nc.dram_tensor("x1_o", [MTOK, D], F32, kind="ExternalOutput").ap()
    xn1_o = nc.dram_tensor("xn1_o", [MTOK, D], F32, kind="ExternalOutput").ap()
    delta_o = nc.dram_tensor("delta_o", [MTOK, D], F32, kind="ExternalOutput").ap()
    dx_o = nc.dram_tensor("dx_o", [MTOK, D], F32, kind="ExternalOutput").ap()
    Bssm_o = nc.dram_tensor("Bssm_o", [MTOK, NS], F32, kind="ExternalOutput").ap()
    Cssm_o = nc.dram_tensor("Cssm_o", [MTOK, NS], F32, kind="ExternalOutput").ap()

    hs_r = hs.rearrange("(tt p) d -> tt p d", p=128)
    xn0_r = xn0s.rearrange("(tt p) d -> tt p d", p=128)
    x0_r = x0s.rearrange("(tt p) d -> tt p d", p=128)
    ffn_r = ffn0s.rearrange("(tt p) d -> tt p d", p=128)
    x1_r = x1_o.rearrange("(tt p) d -> tt p d", p=128)
    xn1_r = xn1_o.rearrange("(tt p) d -> tt p d", p=128)
    del_r = delta_o.rearrange("(tt p) d -> tt p d", p=128)
    dx_r = dx_o.rearrange("(tt p) d -> tt p d", p=128)
    Bs_r = Bssm_o.rearrange("(tt p) d -> tt p d", p=128)
    Cs_r = Cssm_o.rearrange("(tt p) d -> tt p d", p=128)

    with ExitStack() as es:
        p = Prog(nc, es)
        for s in ("ld", "st", "ts", "vs", "ss"):
            p.sem(s)

        spw = sbuf(nc, es, "spw", [128, KT * D], BF16)
        xpw = sbuf(nc, es, "xpw", [128, KT * 384], BF16)
        dtw = sbuf(nc, es, "dtw", [128, D], BF16)
        lnwb = sbuf(nc, es, "lnwb", [128, D], BF16)
        lnbb = sbuf(nc, es, "lnbb", [128, D], BF16)
        nwb = sbuf(nc, es, "nwb", [128, D], BF16)
        nwc = sbuf(nc, es, "nwc", [128, KT], F32)
        spbb = sbuf(nc, es, "spbb", [128, D], BF16)
        sgbb = sbuf(nc, es, "sgbb", [128, D], BF16)
        dtbb = sbuf(nc, es, "dtbb", [128, D], BF16)
        idb = sbuf(nc, es, "idb", [128, 128], BF16)
        tA = sbuf(nc, es, "tA", [128, D], F32)
        tB = sbuf(nc, es, "tB", [128, D], F32)
        sA = sbuf(nc, es, "sA", [128, D], F32)
        sB = sbuf(nc, es, "sB", [128, D], F32)
        sC = sbuf(nc, es, "sC", [128, D], F32)
        sD = sbuf(nc, es, "sD", [128, D], F32)
        sG = sbuf(nc, es, "sG", [128, D], F32)
        xbb = sbuf(nc, es, "xbb", [128, D], BF16)
        fmb = sbuf(nc, es, "fmb", [128, KT * 128], BF16)
        fmb2 = sbuf(nc, es, "fmb2", [128, KT * 128], BF16)
        rkb = sbuf(nc, es, "rkb", [128, 128], BF16)
        sE = sbuf(nc, es, "sE", [128, 384], F32)
        mask_t = sbuf(nc, es, "maskd", [128, D], U8)
        red = sbuf(nc, es, "redd", [128, 1], F32)
        red2 = sbuf(nc, es, "redd2", [128, 1], F32)
        rrms = sbuf(nc, es, "rrms", [128, 1], F32)

        pG = [psum(nc, es, f"pGD{i}", [128, 512], F32) for i in range(3)]
        pTr = psum(nc, es, "pTrD", [128, 128], BF16)
        pX = psum(nc, es, "pXD", [128, 384], F32)

        reg_act_consts(nc, es, p, cst, [-1.0, 0.5])
        p.dma("gpsimd", "ld", lambda g: g.dma_start(out=lnwb[:], in_=lnw.partition_broadcast(128)))
        p.dma("gpsimd", "ld", lambda g: g.dma_start(out=lnbb[:], in_=lnb.partition_broadcast(128)))
        p.dma("gpsimd", "ld", lambda g: g.dma_start(out=nwb[:], in_=nwr.partition_broadcast(128)))
        p.dma("gpsimd", "ld", lambda g: g.dma_start(out=nwc[:], in_=nwcol))
        p.dma("gpsimd", "ld", lambda g: g.dma_start(out=spbb[:], in_=spbr.partition_broadcast(128)))
        p.dma("gpsimd", "ld", lambda g: g.dma_start(out=sgbb[:], in_=sgr.partition_broadcast(128)))
        p.dma("gpsimd", "ld", lambda g: g.dma_start(out=dtbb[:], in_=dtbr.partition_broadcast(128)))
        p.dma("gpsimd", "ld", lambda g: g.dma_start(out=idb[:], in_=iden))
        p.dma("gpsimd", "ld", lambda g: g.dma_start(out=dtw[:], in_=dtwT))
        for kt in range(KT):
            p.dma("gpsimd", "ld", lambda g, kt=kt: g.dma_start(out=xpw[:, kt * 384:(kt + 1) * 384], in_=xpwTb[kt]))
        for kt in range(KT):
            p.dma("gpsimd", "ld", lambda g, kt=kt: g.dma_start(out=spw[:, kt * D:(kt + 1) * D], in_=spwTb[kt]))
        p.wait("vector", "ld")
        for kt in range(KT):
            p.vop(lambda v, kt=kt: v.tensor_scalar(out=spw[:, kt * D:(kt + 1) * D], in0=spw[:, kt * D:(kt + 1) * D], scalar1=nwc[:, kt:kt + 1], scalar2=None, op0=ALU.mult))

        def ln_chain(src, dst, eps):
            p.vop(lambda v: v.reduce_sum(out=red[:], in_=src[:], axis=AX.X))
            p.vop(lambda v: v.tensor_scalar(out=red[:], in0=red[:], scalar1=1.0 / D, scalar2=None, op0=ALU.mult))
            p.vop(lambda v: v.tensor_scalar(out=dst[:], in0=src[:], scalar1=red[:], scalar2=None, op0=ALU.subtract))
            p.wait("scalar", "vs")
            p.sop(lambda s: s.activation(out=tB[:], in_=dst[:], func=AF.Square, accum_out=red2[:]))
            p.wait("vector", "ss")
            p.vop(lambda v: v.tensor_scalar(out=red2[:], in0=red2[:], scalar1=1.0 / D, scalar2=eps, op0=ALU.mult, op1=ALU.add))
            p.wait("scalar", "vs")
            p.sop(lambda s: s.activation(out=red2[:], in_=red2[:], func=AF.Sqrt))
            p.wait("vector", "ss")
            p.vop(lambda v: v.reciprocal(out=red2[:], in_=red2[:]))
            p.vop(lambda v: v.tensor_scalar(out=dst[:], in0=dst[:], scalar1=red2[:], scalar2=None, op0=ALU.mult))

        def transpose16(srcb, dstfm):
            p.wait("tensor", "vs")
            for kt in range(KT):
                p.op("tensor", "ts", lambda te, kt=kt: te.transpose(pTr[:], srcb[:, kt * 128:(kt + 1) * 128], idb[:]))
                p.wait("vector", "ts")
                p.vop(lambda v, kt=kt: v.tensor_copy(out=dstfm[:, kt * 128:(kt + 1) * 128], in_=pTr[:]))
                p.wait("tensor", "vs")

        for tt in range(TT):
            p.wait("gpsimd", "vs")
            p.wait("gpsimd", "ss")
            p.dma("gpsimd", "ld", lambda g, tt=tt: g.dma_start(out=tA[:], in_=hs_r[tt]))
            p.dma("gpsimd", "ld", lambda g, tt=tt: g.dma_start(out=sA[:], in_=xn0_r[tt]))
            p.wait("vector", "ld")
            p.wait("vector", "st")
            p.vop(lambda v: v.tensor_add(out=sA[:], in0=sA[:], in1=tA[:]))
            ln_chain(sA, sB, LN_EPS)
            p.vop(lambda v: v.tensor_tensor(out=sB[:], in0=sB[:], in1=lnwb[:], op=ALU.mult))
            p.op("vector", "vs", lambda v: v.tensor_tensor(out=sB[:], in0=sB[:], in1=lnbb[:], op=ALU.add))
            p.wait("gpsimd", "vs")
            p.dma("gpsimd", "ld", lambda g, tt=tt: g.dma_start(out=tA[:], in_=x0_r[tt]))
            p.dma("gpsimd", "ld", lambda g, tt=tt: g.dma_start(out=sC[:], in_=ffn_r[tt]))
            p.wait("vector", "ld")
            p.vop(lambda v: v.tensor_add(out=sA[:], in0=sB[:], in1=tA[:]))
            p.vop(lambda v: v.tensor_add(out=sA[:], in0=sA[:], in1=sC[:]))
            p.wait("sync", "vs")
            p.dma("sync", "st", lambda sy, tt=tt: sy.dma_start(out=x1_r[tt], in_=sA[:]))
            ln_chain(sA, sB, LN_EPS)
            p.op("vector", "vs", lambda v: v.engine_nop())
            p.wait("sync", "vs")
            p.dma("sync", "st", lambda sy, tt=tt: sy.dma_start(out=xn1_r[tt], in_=sB[:]))
            p.wait("scalar", "vs")
            p.sop(lambda s: s.activation(out=tB[:], in_=sB[:], func=AF.Square, accum_out=rrms[:]))
            p.wait("vector", "ss")
            p.vop(lambda v: v.tensor_scalar(out=rrms[:], in0=rrms[:], scalar1=1.0 / D, scalar2=1e-6, op0=ALU.mult, op1=ALU.add))
            p.wait("scalar", "vs")
            p.sop(lambda s: s.activation(out=rrms[:], in_=rrms[:], func=AF.Sqrt))
            p.wait("vector", "ss")
            p.vop(lambda v: v.reciprocal(out=rrms[:], in_=rrms[:]))
            p.vop(lambda v: v.tensor_scalar(out=sC[:], in0=sB[:], scalar1=rrms[:], scalar2=None, op0=ALU.mult))
            p.vop(lambda v: v.tensor_tensor(out=sC[:], in0=sC[:], in1=nwb[:], op=ALU.mult))
            p.vop(lambda v: v.tensor_copy(out=xbb[:], in_=sB[:]))
            transpose16(xbb, fmb)
            for nch in range(NCH):
                bank = pG[nch % 3]
                p.wait("tensor", "vs")
                p.op("tensor", "ts", gemm16(
                    bank,
                    lambda kt: fmb[:, kt * 128:(kt + 1) * 128],
                    lambda kt, nch=nch: spw[:, kt * D + nch * 512: kt * D + (nch + 1) * 512]))
                p.wait("vector", "ts")
                p.vop(lambda v, bank=bank, nch=nch: v.tensor_scalar(
                    out=sD[:, nch * 512:(nch + 1) * 512], in0=bank[:], scalar1=rrms[:], scalar2=None, op0=ALU.mult))
            p.vop(lambda v: v.tensor_tensor(out=sD[:], in0=sD[:], in1=spbb[:], op=ALU.add))
            p.op("vector", "vs", lambda v: v.tensor_tensor(out=sD[:], in0=sD[:], in1=sgbb[:], op=ALU.mult))
            p.wait("scalar", "vs")
            p.sop(lambda s: s.activation(out=sD[:], in_=sD[:], func=AF.Sigmoid))
            p.wait("vector", "ss")
            p.vop(lambda v: v.tensor_tensor(out=sC[:], in0=sC[:], in1=sD[:], op=ALU.mult))
            p.vop(lambda v: v.tensor_copy(out=xbb[:], in_=sC[:]))
            transpose16(xbb, fmb2)
            p.wait("tensor", "vs")
            p.op("tensor", "ts", lambda te: [te.matmul(pX[:], fmb2[:, kt * 128:(kt + 1) * 128],
                                                       xpw[:, kt * 384:(kt + 1) * 384],
                                                       start=(kt == 0), stop=(kt == KT - 1))
                                             for kt in range(KT)][-1])
            p.wait("vector", "ts")
            p.op("vector", "vs", lambda v: v.tensor_copy(out=sE[:], in_=pX[:]))
            p.wait("sync", "vs")
            p.dma("sync", "st", lambda sy, tt=tt: sy.dma_start(out=Bs_r[tt], in_=sE[:, 128:256]))
            p.dma("sync", "st", lambda sy, tt=tt: sy.dma_start(out=Cs_r[tt], in_=sE[:, 256:384]))
            p.vop(lambda v: v.tensor_copy(out=xbb[:, 0:128], in_=sE[:, 0:128]))
            p.wait("tensor", "vs")
            p.op("tensor", "ts", lambda te: te.transpose(pTr[:], xbb[:, 0:128], idb[:]))
            p.wait("vector", "ts")
            p.op("vector", "vs", lambda v: v.tensor_copy(out=rkb[:], in_=pTr[:]))
            p.wait("tensor", "vs")
            for nch in range(NCH):
                bank = pG[nch % 3]
                p.wait("tensor", "vs")
                p.op("tensor", "ts", lambda te, bank=bank, nch=nch: te.matmul(
                    bank[:], rkb[:], dtw[:, nch * 512:(nch + 1) * 512], start=True, stop=True))
                p.wait("vector", "ts")
                p.vop(lambda v, bank=bank, nch=nch: v.tensor_tensor(
                    out=sG[:, nch * 512:(nch + 1) * 512], in0=bank[:],
                    in1=dtbb[:, nch * 512:(nch + 1) * 512], op=ALU.add))
            p.op("vector", "vs", lambda v: v.engine_nop())
            p.wait("scalar", "vs")
            p.sop(lambda s: s.activation(out=tB[:], in_=sG[:], func=AF.Abs))
            p.sop(lambda s: s.activation(out=tB[:], in_=tB[:], func=AF.Exp, scale=-1.0))
            p.sop(lambda s: s.activation(out=tB[:], in_=tB[:], func=AF.Ln, bias=1.0))
            p.wait("vector", "ss")
            p.vop(lambda v: v.scalar_tensor_tensor(out=sG[:], in0=sG[:], scalar=0.0, in1=tB[:], op0=ALU.max, op1=ALU.add))
            p.op("vector", "vs", lambda v: v.tensor_mul(out=sD[:], in0=sG[:], in1=sC[:]))
            p.wait("sync", "vs")
            p.dma("sync", "st", lambda sy, tt=tt: sy.dma_start(out=del_r[tt], in_=sG[:]))
            p.dma("sync", "st", lambda sy, tt=tt: sy.dma_start(out=dx_r[tt], in_=sD[:]))

        p.wait("sync", "st")
        block = es.enter_context(nc.Block())
        p.attach(block)
    return nc


def run_D(inputs, h_tm, xn0, x0, ffn0):
    if "D" not in _CACHE:
        _CACHE["D"] = build_D()
    nw = inputs["L1_attn_norm_w"].reshape(1, D)
    maps = []
    spwTb = np.ascontiguousarray(inputs["L1_attn_select_proj_w"].T.reshape(KT, 128, D))
    xpwTb = np.ascontiguousarray(inputs["L1_attn_x_proj_w"].T.reshape(KT, 128, 384))
    dtwT = np.ascontiguousarray(inputs["L1_attn_dt_proj_w"].T)
    nwcol = np.ascontiguousarray(nw.reshape(KT, 128).T)
    for c in range(8):
        sl = slice(c * MTOK, (c + 1) * MTOK)
        maps.append(dict(
            hs=np.ascontiguousarray(h_tm[sl]), xn0s=np.ascontiguousarray(xn0[sl]),
            x0s=np.ascontiguousarray(x0[sl]), ffn0s=np.ascontiguousarray(ffn0[sl]),
            lnw=inputs["L0_attn_ln_w"].reshape(1, D), lnb=inputs["L0_attn_ln_b"].reshape(1, D),
            nwr=nw, nwcol=nwcol,
            spbr=inputs["L1_attn_select_proj_b"].reshape(1, D),
            sgr=inputs["L1_attn_select_gate"].reshape(1, D),
            dtbr=inputs["L1_attn_dt_proj_b"].reshape(1, D),
            spwTb=spwTb, xpwTb=xpwTb, dtwT=dtwT,
            iden=np.eye(128, dtype=np.float32),
            cst=np.array([[-1.0, 0.5]], np.float32),
        ))
    res = run_bass_kernel_spmd(_CACHE["D"], maps, core_ids=list(range(8)))
    return res.results





DTC = 8
TCH = 1024
NTC = T // TCH


def build_E():
    nc = bass.Bass("TRN2", target_bir_lowering=False)
    dfm = nc.dram_tensor("dfm", [DTC, 128, T], F32, kind="ExternalInput").ap()
    xfm = nc.dram_tensor("xfm", [DTC, 128, T], F32, kind="ExternalInput").ap()
    BTb = nc.dram_tensor("BTb", [NS, T], F32, kind="ExternalInput").ap()
    CTb = nc.dram_tensor("CTb", [NS, T], F32, kind="ExternalInput").ap()
    Alg = nc.dram_tensor("Alg", [DTC, 128, NS], F32, kind="ExternalInput").ap()
    y_o = nc.dram_tensor("y_o", [DTC, 128, T], F32, kind="ExternalOutput").ap()
    GD = 4

    with ExitStack() as es:
        p = Prog(nc, es)
        for s in ("ld", "st", "ts", "vs", "ss", "lb0", "lb1"):
            p.sem(s)
        At = sbuf(nc, es, "At", [128, DTC * NS], F32)
        dC = sbuf(nc, es, "dC", [128, GD * T], BF16)
        xC = sbuf(nc, es, "xC", [128, GD * T], BF16)
        yak = sbuf(nc, es, "yak", [128, GD * T], F32)
        Bbb = [sbuf(nc, es, f"Bbb{i}", [128, T], BF16) for i in range(2)]
        Cbb = [sbuf(nc, es, f"Cbb{i}", [128, T], BF16) for i in range(2)]
        ab = [sbuf(nc, es, f"ab{i}", [128, T], BF16) for i in range(2)]
        ub = sbuf(nc, es, "ub", [128, T], BF16)
        hb = sbuf(nc, es, "hb", [128, T], BF16)
        zb = sbuf(nc, es, "zb", [128, T], BF16)

        for dt in range(DTC):
            p.dma("gpsimd", "ld", lambda g, dt=dt: g.dma_start(out=At[:, dt * NS:(dt + 1) * NS], in_=Alg[dt]))
        p.wait("scalar", "ld")
        p.sop(lambda s: s.activation(out=At[:], in_=At[:], func=AF.Exp))
        p.wait("vector", "ss")
        p.vop(lambda v: v.tensor_scalar(out=At[:], in0=At[:], scalar1=-1.0, scalar2=None, op0=ALU.mult))
        p.wait("scalar", "vs")

        ab_last_vs = [0, 0]
        bc_used_vs = [0, 0]
        for grp in range(DTC // GD):
            p.wait("gpsimd", "vs")
            p.wait("gpsimd", "ss")
            p.wait("gpsimd", "st")
            for i in range(GD):
                dt = grp * GD + i
                p.dma("gpsimd", "ld", lambda g, dt=dt, i=i: g.dma_start(out=dC[:, i * T:(i + 1) * T], in_=dfm[dt]))
                p.dma("gpsimd", "ld", lambda g, dt=dt, i=i: g.dma_start(out=xC[:, i * T:(i + 1) * T], in_=xfm[dt]))
            p.wait("scalar", "ld")
            p.wait("vector", "st")
            for n in range(NS):
                par = n % 2
                p.wait("gpsimd", "vs", bc_used_vs[par])
                lbs = "lb0" if par == 0 else "lb1"
                p.dma2("gpsimd", lbs, lambda g, n=n, par=par: g.dma_start(out=Bbb[par][:], in_=BTb[n:n + 1, :].partition_broadcast(128)))
                p.dma2("gpsimd", lbs, lambda g, n=n, par=par: g.dma_start(out=Cbb[par][:], in_=CTb[n:n + 1, :].partition_broadcast(128)))
                p.wait("vector", lbs)
                for i in range(GD):
                    dt = grp * GD + i
                    apar = (n * GD + i) % 2
                    col = dt * NS + n
                    p.wait("scalar", "vs", ab_last_vs[apar])
                    p.op("scalar", "ss", lambda s, i=i, col=col, apar=apar: s.activation(
                        out=ab[apar][:], in_=dC[:, i * T:(i + 1) * T], func=AF.Exp,
                        scale=At[:, col:col + 1]))
                    p.wait("vector", "ss")
                    p.vop(lambda v, i=i, par=par: v.tensor_mul(
                        out=ub[:], in0=xC[:, i * T:(i + 1) * T], in1=Bbb[par][:]))
                    p.vop(lambda v, apar=apar: v.tensor_tensor_scan(
                        out=hb[:], data0=ab[apar][:], data1=ub[:],
                        initial=0.0, op0=ALU.mult, op1=ALU.add))
                    ab_last_vs[apar] = p.cnt["vs"]
                    p.vop(lambda v, par=par: v.tensor_mul(out=zb[:], in0=hb[:], in1=Cbb[par][:]))
                    if n == 0:
                        p.vop(lambda v, i=i: v.tensor_copy(out=yak[:, i * T:(i + 1) * T], in_=zb[:]))
                    else:
                        p.vop(lambda v, i=i: v.tensor_add(
                            out=yak[:, i * T:(i + 1) * T],
                            in0=yak[:, i * T:(i + 1) * T], in1=zb[:]))
                bc_used_vs[par] = p.cnt["vs"]
            p.wait("sync", "vs")
            for i in range(GD):
                dt = grp * GD + i
                p.dma("sync", "st", lambda sy, dt=dt, i=i: sy.dma_start(out=y_o[dt], in_=yak[:, i * T:(i + 1) * T]))
        p.wait("sync", "st")
        block = es.enter_context(nc.Block())
        p.attach(block)
    return nc


def run_E(inputs, delta_tm, dx_tm, Bssm, Cssm):
    if "E" not in _CACHE:
        _CACHE["E"] = build_E()
    Alog = inputs["L1_attn_A_log"]
    d4 = delta_tm.reshape(B, T, D)
    x4 = dx_tm.reshape(B, T, D)
    B4 = Bssm.reshape(B, T, NS)
    C4 = Cssm.reshape(B, T, NS)
    maps = []
    for c in range(8):
        b, j = c // 2, c % 2
        dsl = slice(j * 1024, (j + 1) * 1024)
        maps.append(dict(
            dfm=np.ascontiguousarray(d4[b].T[dsl]).reshape(DTC, 128, T),
            xfm=np.ascontiguousarray(x4[b].T[dsl]).reshape(DTC, 128, T),
            BTb=np.ascontiguousarray(B4[b].T),
            CTb=np.ascontiguousarray(C4[b].T),
            Alg=np.ascontiguousarray(Alog[dsl]).reshape(DTC, 128, NS),
        ))
    res = run_bass_kernel_spmd(_CACHE["E"], maps, core_ids=list(range(8)))
    return res.results





def build_F():
    nc = bass.Bass("TRN2", target_bir_lowering=False)
    ys = nc.dram_tensor("ys", [MTOK, D], F32, kind="ExternalInput").ap()
    xn1s = nc.dram_tensor("xn1s", [MTOK, D], F32, kind="ExternalInput").ap()
    x1s = nc.dram_tensor("x1s", [MTOK, D], F32, kind="ExternalInput").ap()
    dpr = nc.dram_tensor("dpr", [1, D], F32, kind="ExternalInput").ap()
    opwTb = nc.dram_tensor("opwTb", [KT, 128, D], F32, kind="ExternalInput").ap()
    wrb = nc.dram_tensor("wrb", [KT, 128, E], F32, kind="ExternalInput").ap()
    we1b = nc.dram_tensor("we1b", [E, KT, 128, HE], F32, kind="ExternalInput").ap()
    we3b = nc.dram_tensor("we3b", [E, KT, 128, HE], F32, kind="ExternalInput").ap()
    we2b = nc.dram_tensor("we2b", [E, NCH, 8, 128, 512], F32, kind="ExternalInput").ap()
    ws1b = nc.dram_tensor("ws1b", [8, KT, 128, 512], F32, kind="ExternalInput").ap()
    ws3b = nc.dram_tensor("ws3b", [8, KT, 128, 512], F32, kind="ExternalInput").ap()
    ws2b = nc.dram_tensor("ws2b", [NCH, 32, 128, 512], F32, kind="ExternalInput").ap()
    iden = nc.dram_tensor("iden", [128, 128], F32, kind="ExternalInput").ap()
    out_o = nc.dram_tensor("out_o", [MTOK, D], F32, kind="ExternalOutput").ap()
    at1_hbm = nc.dram_tensor("at1_hbm", [TT, 128, D], F32)
    moe_hbm = nc.dram_tensor("moe_hbm", [TT, 128, D], F32)

    ys_r = ys.rearrange("(tt p) d -> tt p d", p=128)
    xn1_r = xn1s.rearrange("(tt p) d -> tt p d", p=128)
    x1_r = x1s.rearrange("(tt p) d -> tt p d", p=128)
    out_r = out_o.rearrange("(tt p) d -> tt p d", p=128)

    with ExitStack() as es:
        p = Prog(nc, es)
        for s in ("lw", "lx", "st", "ts", "vs", "ss"):
            p.sem(s)
        wbig = sbuf(nc, es, "wbig", [128, 2 * KT * HE], BF16)
        U1 = sbuf(nc, es, "U1", [128, 8 * D], F32)
        xnh = sbuf(nc, es, "xnh", [128, KT * 1024], BF16)
        we2s = sbuf(nc, es, "we2s", [128, 8 * 512], BF16)
        sY = sbuf(nc, es, "sY", [128, 512], F32)
        gbuf = sbuf(nc, es, "gbuf", [128, 8 * 512], BF16)
        wrt = sbuf(nc, es, "wrt", [128, KT * E], BF16)
        dpb = sbuf(nc, es, "dpb", [128, D], BF16)
        idb = sbuf(nc, es, "idb", [128, 128], BF16)
        sA = sbuf(nc, es, "sA", [128, 512], F32)
        ybb = sbuf(nc, es, "ybb", [128, D], BF16)
        xbb = sbuf(nc, es, "xbb", [128, D], BF16)
        fmb = sbuf(nc, es, "fmb", [128, KT * 128], BF16)
        sgb = sbuf(nc, es, "sgb", [128, 512], BF16)
        shb = sbuf(nc, es, "shb", [128, 512], BF16)
        mo = sbuf(nc, es, "mo", [128, 512], F32)
        sX = sbuf(nc, es, "sX", [128, 512], F32)
        cw_all = sbuf(nc, es, "cw_all", [128, TT * E], F32)
        prb = sbuf(nc, es, "prb", [128, E], F32)
        pwk = sbuf(nc, es, "pwk", [128, E], F32)
        msk = sbuf(nc, es, "msk", [128, E], U8)
        ninf = sbuf(nc, es, "ninf", [128, E], F32)
        zer = sbuf(nc, es, "zer", [128, E], F32)
        red = sbuf(nc, es, "redf", [128, 1], F32)
        pG = [psum(nc, es, f"pGF{i}", [128, 512], F32) for i in range(6)]
        pTr = [psum(nc, es, f"pTrF{i}", [128, 128], BF16) for i in range(2)]

        p.dma("gpsimd", "lx", lambda g: g.dma_start(out=dpb[:], in_=dpr.partition_broadcast(128)))
        p.dma("gpsimd", "lx", lambda g: g.dma_start(out=idb[:], in_=iden))
        for kt in range(KT):
            p.dma("gpsimd", "lx", lambda g, kt=kt: g.dma_start(out=wrt[:, kt * E:(kt + 1) * E], in_=wrb[kt]))
            p.dma2("gpsimd", "lw", lambda g, kt=kt: g.dma_start(out=wbig[:, kt * D:(kt + 1) * D], in_=opwTb[kt]))
        p.vop(lambda v: v.memset(ninf[:], -3e38))
        p.vop(lambda v: v.memset(zer[:], 0.0))

        def transpose16(srcb, dstfm, colf, ntile=KT):
            p.wait("tensor", "vs")
            for kt in range(ntile):
                b = pTr[kt % 2]
                p.op("tensor", "ts", lambda te, kt=kt, b=b: te.transpose(b[:], srcb[:, kt * 128:(kt + 1) * 128], idb[:]))
                p.wait("vector", "ts")
                p.vop(lambda v, kt=kt, b=b: v.tensor_copy(out=dstfm[:, colf(kt):colf(kt) + 128], in_=b[:]))
                if kt + 1 < ntile:
                    p.wait("tensor", "vs", p.cnt["vs"] - 1)

        p.wait("tensor", "lw")
        p.wait("tensor", "lx")
        for tt in range(TT):
            p.wait("gpsimd", "vs")
            p.wait("gpsimd", "st")
            p.wait("gpsimd", "ts")
            p.dma("gpsimd", "lx", lambda g, tt=tt: g.dma_start(out=ybb[:], in_=ys_r[tt]))
            p.dma("gpsimd", "lx", lambda g, tt=tt: g.dma_start(out=xbb[:], in_=xn1_r[tt]))
            p.wait("vector", "lx")
            p.wait("vector", "st")
            p.wait("tensor", "lx")
            transpose16(xbb, fmb, lambda kt: kt * 128)
            p.wait("tensor", "vs")
            def router_mm(te, tt=tt):
                for kt in range(KT):
                    ins = te.matmul(pG[2][:, 0:E],
                                    fmb[:, kt * 128:(kt + 1) * 128],
                                    wrt[:, kt * E:(kt + 1) * E],
                                    start=(kt == 0), stop=(kt == KT - 1))
                return ins
            p.op("tensor", "ts", router_mm)
            p.wait("vector", "ts")
            p.vop(lambda v: v.tensor_copy(out=prb[:], in_=pG[2][:, 0:E]))
            p.vop(lambda v: v.reduce_max(out=red[:], in_=prb[:], axis=AX.X))
            p.vop(lambda v: v.tensor_scalar(out=red[:], in0=red[:], scalar1=-1.0, scalar2=None, op0=ALU.mult))
            p.wait("scalar", "vs")
            p.sop(lambda s: s.activation(out=prb[:], in_=prb[:], func=AF.Exp, bias=red[:]))
            p.wait("vector", "ss")
            p.vop(lambda v: v.reduce_sum(out=red[:], in_=prb[:], axis=AX.X))
            p.vop(lambda v: v.reciprocal(out=red[:], in_=red[:]))
            p.vop(lambda v: v.tensor_scalar(out=prb[:], in0=prb[:], scalar1=red[:], scalar2=None, op0=ALU.mult))
            p.vop(lambda v: v.tensor_copy(out=pwk[:], in_=prb[:]))
            for _ in range(3):
                p.vop(lambda v: v.reduce_max(out=red[:], in_=pwk[:], axis=AX.X))
                p.vop(lambda v: v.tensor_scalar(out=msk[:], in0=pwk[:], scalar1=red[:], scalar2=None, op0=ALU.is_ge))
                p.vop(lambda v: v.select(out=pwk[:], mask=msk[:], on_true=ninf[:], on_false=pwk[:]))
            p.vop(lambda v: v.reduce_max(out=red[:], in_=pwk[:], axis=AX.X))
            p.vop(lambda v: v.tensor_scalar(out=msk[:], in0=prb[:], scalar1=red[:], scalar2=None, op0=ALU.is_ge))
            p.vop(lambda v: v.select(out=pwk[:], mask=msk[:], on_true=prb[:], on_false=zer[:]))
            p.vop(lambda v: v.reduce_sum(out=red[:], in_=pwk[:], axis=AX.X))
            p.vop(lambda v: v.reciprocal(out=red[:], in_=red[:]))
            p.vop(lambda v, tt=tt: v.tensor_scalar(out=cw_all[:, tt * E:(tt + 1) * E], in0=pwk[:],
                                                   scalar1=red[:], scalar2=0.8, op0=ALU.mult, op1=ALU.mult))
            p.wait("vector", "ts")
            p.vop(lambda v: v.tensor_tensor(out=xbb[:], in0=xbb[:], in1=dpb[:], op=ALU.mult))
            p.vop(lambda v: v.tensor_add(out=xbb[:], in0=xbb[:], in1=ybb[:]))
            transpose16(xbb, fmb, lambda kt: kt * 128)
            for nch in range(NCH):
                bank = pG[nch % 3]
                p.wait("tensor", "vs")
                p.op("tensor", "ts", gemm16(
                    bank,
                    lambda kt: fmb[:, kt * 128:(kt + 1) * 128],
                    lambda kt, nch=nch: wbig[:, kt * D + nch * 512: kt * D + (nch + 1) * 512]))
                p.wait("vector", "ts")
                p.wait("vector", "st")
                p.vop(lambda v, bank=bank: v.tensor_copy(out=sA[:], in_=bank[:]))
                p.wait("sync", "vs")
                p.dma("sync", "st", lambda sy, tt=tt, nch=nch: sy.dma_start(
                    out=at1_hbm.ap()[tt][:, nch * 512:(nch + 1) * 512], in_=sA[:]))

        moe = U1
        for th in range(2):
            tts = [th * 8 + i for i in range(8)]
            for i, tt in enumerate(tts):
                p.wait("gpsimd", "vs")
                p.wait("gpsimd", "ts")
                p.dma("gpsimd", "lx", lambda g, tt=tt: g.dma_start(out=xbb[:], in_=xn1_r[tt]))
                p.wait("vector", "lx")
                p.wait("tensor", "lx")
                transpose16(xbb, xnh, lambda kt, i=i: kt * 1024 + i * 128)
            for e in range(E):
                p.wait("gpsimd", "ts")
                for kt in range(KT):
                    p.dma2("gpsimd", "lw", lambda g, e=e, kt=kt: g.dma_start(out=wbig[:, kt * HE:(kt + 1) * HE], in_=we1b[e, kt]))
                    p.dma2("gpsimd", "lw", lambda g, e=e, kt=kt: g.dma_start(out=wbig[:, KT * HE + kt * HE: KT * HE + (kt + 1) * HE], in_=we3b[e, kt]))
                p.wait("tensor", "lw")
                for tokch in range(2):
                    up_evac_vs = [p.cnt["vs"]] * 2
                    for ht in range(8):
                        hp = ht % 2
                        bg, bl = pG[2 * hp], pG[2 * hp + 1]
                        p.wait("tensor", "vs", up_evac_vs[hp])
                        p.op("tensor", "ts", gemm16(
                            bg,
                            lambda kt, ht=ht: wbig[:, kt * HE + ht * 128: kt * HE + (ht + 1) * 128],
                            lambda kt, tokch=tokch: xnh[:, kt * 1024 + tokch * 512: kt * 1024 + (tokch + 1) * 512]))
                        p.op("tensor", "ts", gemm16(
                            bl,
                            lambda kt, ht=ht: wbig[:, KT * HE + kt * HE + ht * 128: KT * HE + kt * HE + (ht + 1) * 128],
                            lambda kt, tokch=tokch: xnh[:, kt * 1024 + tokch * 512: kt * 1024 + (tokch + 1) * 512]))
                        p.wait("scalar", "ts")
                        p.wait("scalar", "vs")
                        p.sop(lambda s, bg=bg: s.activation(out=sgb[:], in_=bg[:], func=AF.Silu))
                        p.wait("vector", "ts")
                        p.wait("vector", "ss")
                        p.vop(lambda v, ht=ht, bl=bl: v.tensor_copy(
                            out=gbuf[:, ht * 512:(ht + 1) * 512], in_=bl[:]))
                        p.vop(lambda v, ht=ht: v.tensor_mul(
                            out=gbuf[:, ht * 512:(ht + 1) * 512],
                            in0=sgb[:], in1=gbuf[:, ht * 512:(ht + 1) * 512]))
                        up_evac_vs[hp] = p.cnt["vs"]
                    for dch in range(NCH):
                        p.wait("gpsimd", "ts")
                        for ht in range(8):
                            p.dma2("gpsimd", "lw", lambda g, e=e, dch=dch, ht=ht: g.dma_start(
                                out=we2s[:, ht * 512:(ht + 1) * 512], in_=we2b[e, dch, ht]))
                        p.wait("tensor", "lw")
                        dn_evac_vs = [p.cnt["vs"]] * 2
                        for tb in range(4):
                            q = tokch * 4 + tb
                            col = (th * 8 + q) * E + e
                            bp = pG[4 + tb % 2]
                            p.wait("tensor", "vs", dn_evac_vs[tb % 2])
                            p.op("tensor", "ts", gemm16(
                                bp,
                                lambda ht, tb=tb: gbuf[:, ht * 512 + tb * 128: ht * 512 + (tb + 1) * 128],
                                lambda ht: we2s[:, ht * 512:(ht + 1) * 512], nk=8))
                            p.wait("vector", "ts")
                            if e == 0:
                                p.vop(lambda v, q=q, dch=dch, col=col, bp=bp: v.tensor_scalar(
                                    out=moe[:, q * D + dch * 512: q * D + (dch + 1) * 512],
                                    in0=bp[:], scalar1=cw_all[:, col:col + 1], scalar2=None, op0=ALU.mult))
                            else:
                                p.vop(lambda v, q=q, dch=dch, col=col, bp=bp: v.scalar_tensor_tensor(
                                    out=moe[:, q * D + dch * 512: q * D + (dch + 1) * 512],
                                    in0=bp[:], scalar=cw_all[:, col:col + 1],
                                    in1=moe[:, q * D + dch * 512: q * D + (dch + 1) * 512],
                                    op0=ALU.mult, op1=ALU.add))
                            dn_evac_vs[tb % 2] = p.cnt["vs"]
            p.wait("sync", "vs")
            for i, tt in enumerate(tts):
                p.dma("sync", "st", lambda sy, i=i, tt=tt: sy.dma_start(out=moe_hbm.ap()[tt], in_=moe[:, i * D:(i + 1) * D]))
            shfm = U1.ap().bitcast(BF16)
            p.wait("vector", "st")
            for hsch in range(8):
                p.wait("gpsimd", "ts")
                for kt in range(KT):
                    p.dma2("gpsimd", "lw", lambda g, hsch=hsch, kt=kt: g.dma_start(out=wbig[:, kt * 512:(kt + 1) * 512], in_=ws1b[hsch, kt]))
                    p.dma2("gpsimd", "lw", lambda g, hsch=hsch, kt=kt: g.dma_start(out=wbig[:, 8192 + kt * 512: 8192 + (kt + 1) * 512], in_=ws3b[hsch, kt]))
                p.wait("tensor", "lw")
                for i, tt in enumerate(tts):
                    p.wait("tensor", "vs")
                    p.wait("tensor", "ss")
                    p.op("tensor", "ts", gemm16(
                        pG[0],
                        lambda kt, i=i: xnh[:, kt * 1024 + i * 128: kt * 1024 + (i + 1) * 128],
                        lambda kt: wbig[:, kt * 512:(kt + 1) * 512]))
                    p.op("tensor", "ts", gemm16(
                        pG[1],
                        lambda kt, i=i: xnh[:, kt * 1024 + i * 128: kt * 1024 + (i + 1) * 128],
                        lambda kt: wbig[:, 8192 + kt * 512: 8192 + (kt + 1) * 512]))
                    p.wait("scalar", "ts")
                    p.wait("scalar", "vs")
                    p.sop(lambda s: s.activation(out=sgb[:], in_=pG[0][:], func=AF.Silu))
                    p.wait("vector", "ts")
                    p.wait("vector", "ss")
                    p.vop(lambda v: v.tensor_copy(out=shb[:], in_=pG[1][:]))
                    p.vop(lambda v: v.tensor_mul(out=shb[:], in0=sgb[:], in1=shb[:]))
                    transpose16(shb, shfm, lambda j, i=i, hsch=hsch: i * 4096 + (hsch * 4 + j) * 128, ntile=4)
            for dch in range(NCH):
                p.wait("gpsimd", "ts")
                for hst in range(32):
                    p.dma2("gpsimd", "lw", lambda g, dch=dch, hst=hst: g.dma_start(
                        out=wbig[:, hst * 512:(hst + 1) * 512], in_=ws2b[dch, hst]))
                p.wait("tensor", "lw")
                for i, tt in enumerate(tts):
                    p.wait("gpsimd", "vs")
                    p.dma("gpsimd", "lx", lambda g, tt=tt, dch=dch: g.dma_start(out=mo[:], in_=moe_hbm.ap()[tt][:, dch * 512:(dch + 1) * 512]))
                    p.dma("gpsimd", "lx", lambda g, tt=tt, dch=dch: g.dma_start(out=sX[:], in_=x1_r[tt][:, dch * 512:(dch + 1) * 512]))
                    p.dma("gpsimd", "lx", lambda g, tt=tt, dch=dch: g.dma_start(out=sY[:], in_=at1_hbm.ap()[tt][:, dch * 512:(dch + 1) * 512]))
                    p.wait("tensor", "vs")
                    p.op("tensor", "ts", gemm16(
                        pG[2],
                        lambda hst, i=i: shfm[:, i * 4096 + hst * 128: i * 4096 + (hst + 1) * 128],
                        lambda hst: wbig[:, hst * 512:(hst + 1) * 512], nk=32))
                    p.wait("vector", "ts")
                    p.wait("vector", "lx")
                    p.wait("vector", "st")
                    p.vop(lambda v: v.scalar_tensor_tensor(out=mo[:], in0=pG[2][:], scalar=0.2, in1=mo[:], op0=ALU.mult, op1=ALU.add))
                    p.vop(lambda v: v.tensor_add(out=mo[:], in0=mo[:], in1=sX[:]))
                    p.vop(lambda v: v.tensor_add(out=mo[:], in0=mo[:], in1=sY[:]))
                    p.wait("sync", "vs")
                    p.dma("sync", "st", lambda sy, tt=tt, dch=dch: sy.dma_start(
                        out=out_r[tt][:, dch * 512:(dch + 1) * 512], in_=mo[:]))
        p.wait("sync", "st")
        block = es.enter_context(nc.Block())
        p.attach(block)
    return nc


def run_F(inputs, y_tm, x1, xn1):
    if "F" not in _CACHE:
        _CACHE["F"] = build_F()
    opwTb = np.ascontiguousarray(inputs["L1_attn_out_proj_w"].T.reshape(KT, 128, D))
    wrb = np.ascontiguousarray(inputs["L1_ffn_w_router"].reshape(KT, 128, E))
    we1b = np.ascontiguousarray(inputs["L1_ffn_we1"].reshape(E, KT, 128, HE))
    we3b = np.ascontiguousarray(inputs["L1_ffn_we3"].reshape(E, KT, 128, HE))
    we2b = np.ascontiguousarray(
        inputs["L1_ffn_we2"].reshape(E, 8, 128, NCH, 512).transpose(0, 3, 1, 2, 4))
    ws1b = np.ascontiguousarray(
        inputs["L1_ffn_ws1"].reshape(KT, 128, 8, 512).transpose(2, 0, 1, 3))
    ws3b = np.ascontiguousarray(
        inputs["L1_ffn_ws3"].reshape(KT, 128, 8, 512).transpose(2, 0, 1, 3))
    ws2b = np.ascontiguousarray(
        inputs["L1_ffn_ws2"].reshape(32, 128, NCH, 512).transpose(2, 0, 1, 3))
    maps = []
    for c in range(8):
        sl = slice(c * MTOK, (c + 1) * MTOK)
        maps.append(dict(
            ys=np.ascontiguousarray(y_tm[sl]), xn1s=np.ascontiguousarray(xn1[sl]),
            x1s=np.ascontiguousarray(x1[sl]),
            dpr=inputs["L1_attn_D_param"].reshape(1, D),
            opwTb=opwTb, wrb=wrb, we1b=we1b, we3b=we3b, we2b=we2b,
            ws1b=ws1b, ws3b=ws3b, ws2b=ws2b,
            iden=np.eye(128, dtype=np.float32)))
    res = run_bass_kernel_spmd(_CACHE["F"], maps, core_ids=list(range(8)))
    return res.results


def kernel(**inputs):
    I = inputs
    x0 = np.asarray(I["L0_attn_hidden_states"], np.float32).reshape(B * T, D)

    resA = run_A(I)
    vpr = np.concatenate([np.asarray(resA[c]["vpr_o"]) for c in range(8)], 0).reshape(B, T, D)
    cumP = np.concatenate([np.asarray(resA[c]["cumP_o"]) for c in range(8)], 0).reshape(B, T, D)
    v0 = np.asarray(resA[0]["v0_o"])[0]
    xn0 = np.concatenate([np.asarray(resA[c]["xn0_o"]) for c in range(8)], 0)
    ffn0 = np.concatenate(
        [np.concatenate([np.asarray(resA[c]["ffn0_o"])[i] for i in range(NCH)], 1) for c in range(8)], 0)
    vfull = np.empty((B, TP1, D), np.float32); vfull[:, 0] = v0; vfull[:, 1:] = vpr
    cfull = np.empty((B, TP1, D), np.float32); cfull[:, 0] = 0.0; cfull[:, 1:] = cumP

    resB = run_B(vfull, cfull)
    loghs = [np.asarray(resB[c]["logh_o"]) for c in range(8)]
    parts = np.stack([np.asarray(resB[c]["part_o"])[0] for c in range(8)])

    resC = run_C(loghs, parts)
    h_tm = np.empty((B, T, D), np.float32)
    for c in range(8):
        b, j = c // 2, c % 2
        h_tm[b, :, j * 1024:(j + 1) * 1024] = np.asarray(resC[c]["h_o"]).reshape(1024, T).T
    h_tm = h_tm.reshape(B * T, D)

    resD = run_D(I, h_tm, xn0, x0, ffn0)
    x1 = np.concatenate([np.asarray(resD[c]["x1_o"]) for c in range(8)], 0)
    xn1 = np.concatenate([np.asarray(resD[c]["xn1_o"]) for c in range(8)], 0)
    delta = np.concatenate([np.asarray(resD[c]["delta_o"]) for c in range(8)], 0)
    dx = np.concatenate([np.asarray(resD[c]["dx_o"]) for c in range(8)], 0)
    Bs = np.concatenate([np.asarray(resD[c]["Bssm_o"]) for c in range(8)], 0)
    Cs = np.concatenate([np.asarray(resD[c]["Cssm_o"]) for c in range(8)], 0)

    resE = run_E(I, delta, dx, Bs, Cs)
    y_tm = np.empty((B, T, D), np.float32)
    for c in range(8):
        b, j = c // 2, c % 2
        y_tm[b, :, j * 1024:(j + 1) * 1024] = np.asarray(resE[c]["y_o"]).reshape(1024, T).T
    y_tm = y_tm.reshape(B * T, D)

    resF = run_F(I, y_tm, x1, xn1)
    out = np.concatenate([np.asarray(resF[c]["out_o"]) for c in range(8)], 0)
    return out.reshape(B, T, D).astype(np.float32)
